# version 27
# baseline (speedup 1.0000x reference)
import math
import numpy as np
import ml_dtypes
import concourse.bass as bass
import concourse.mybir as mybir
from concourse.bass_utils import run_bass_kernel_spmd
from contextlib import ExitStack

F32 = mybir.dt.float32
BF16 = mybir.dt.bfloat16
I32 = mybir.dt.int32
AF = mybir.ActivationFunctionType
ALU = mybir.AluOpType
AX = mybir.AxisListType

T = 2048
C = 1024
H = 16
HD = 64
L = 32
FF = 3072
NG = 8
NT = T // 128
NCT = C // 128
NFT = FF // 128
S127 = 1.0 / 127.0
SCL = 1.0 / math.sqrt(L)
EPS = 1e-5
NCORES = 8

ENGS = ("sync", "tensor", "vector", "scalar", "gpsimd")


class Res:
    __slots__ = ("name", "w", "rs", "exclusive")
    def __init__(self, name, exclusive=False):
        self.name = name
        self.w = None
        self.rs = []
        self.exclusive = exclusive


class EM:
    def __init__(self):
        self.ev = {e: [] for e in ENGS}
        self.cnt = {}
        self.owner = {}
        self.queues = {}

    def sem(self, name, owner):
        self.cnt[name] = 0
        self.owner[name] = owner

    def dma_queue(self, name, owner, width=4):
        phys = []
        for i in range(width):
            pn = "%s%d" % (name, i)
            self.sem(pn, "@dma")
            phys.append(pn)
        self.queues[name] = [phys, 0]

    def emit(self, eng, fn, reads=(), writes=(), waits=(), inc=None, amt=1):
        ws = list(waits)
        for r in reads:
            if r.w is not None:
                ws.append(r.w)
            if r.exclusive:
                ws.extend(t for (t, e2) in r.rs if e2 != eng)
        for r in writes:
            if r.w is not None:
                ws.append(r.w)
            ws.extend(t for (t, _e) in r.rs)
        if inc in self.queues:
            q = self.queues[inc]
            inc = q[0][q[1] % len(q[0])]
            q[1] += 1
            if self.cnt[inc]:
                ws.append((inc, self.cnt[inc]))
        tok = None
        if inc is not None:
            assert self.owner[inc] in (eng, "@dma"), (inc, eng)
            self.cnt[inc] += amt
            tok = (inc, self.cnt[inc])
        self.ev[eng].append((fn, tuple(ws), inc, amt))
        for r in reads:
            if tok is not None:
                r.rs.append((tok, eng))
        for r in writes:
            r.w = tok
            r.rs = []
        return tok

    def queue_totals(self, name):
        return [(pn, self.cnt[pn]) for pn in self.queues[name][0]]

    def replay(self, block, sems):
        owner = self.owner
        for engname in ENGS:
            evs = self.ev[engname]
            if not evs:
                continue
            def mk(evs=evs, engname=engname):
                def body(eng):
                    waited = {}
                    for fn, ws, inc, amt in evs:
                        need = {}
                        for (sn, val) in ws:
                            if owner[sn] == engname == "tensor":
                                continue
                            if waited.get(sn, 0) < val and need.get(sn, 0) < val:
                                need[sn] = val
                        for sn, val in need.items():
                            eng.wait_ge(sems[sn], val)
                            waited[sn] = val
                        ins = fn(eng)
                        if inc is not None:
                            ins.then_inc(sems[inc], amt)
                return body
            getattr(block, engname)(mk())



def _rope_tables():
    inv = 1.0 / (10000.0 ** (np.arange(0, L, 2, dtype=np.float64) / L))
    t = np.arange(T, dtype=np.float64)
    fr = np.outer(t, inv)
    cos = np.repeat(np.cos(fr), 2, axis=-1)[:, :L]
    sin = np.repeat(np.sin(fr), 2, axis=-1)[:, :L]
    sinp = sin.copy()
    sinp[:, 0::2] = -sin[:, 0::2]
    return cos.T.astype(np.float32), sinp.T.astype(np.float32)


def _host_prep(inp):
    d = {}
    w = inp["L0_attn_c_attn_w"]
    d["wqkv"] = np.ascontiguousarray(
        w.reshape(NFT, 128, NCT, 128).transpose(0, 3, 2, 1)).reshape(NFT, 128, 1024)
    d["bqkv"] = np.ascontiguousarray(
        inp["L0_attn_c_attn_b"].reshape(NFT, 128).T)
    w = inp["L0_attn_cproj_w"]
    d["wcproj"] = np.ascontiguousarray(
        w.reshape(NCT, 128, NCT, 128).transpose(0, 3, 2, 1)).reshape(NCT, 128, 1024)
    d["bcproj"] = np.ascontiguousarray(
        inp["L0_attn_cproj_b"].reshape(NCT, 128).T)
    for nm, key in (("wg", "L0_ffn_w_gate"), ("wu", "L0_ffn_w_up")):
        w = inp[key]
        d[nm] = np.ascontiguousarray(
            w.reshape(NFT, 128, NCT, 128).transpose(0, 3, 2, 1)).reshape(NFT, 128, 1024)
    w = inp["L0_ffn_w_down"]
    d["wd"] = np.ascontiguousarray(
        w.reshape(NCT, 128, NFT, 128).transpose(0, 3, 2, 1)).reshape(NCT, 128, 3072)

    q2lT = inp["L0_attn_q2l_w"].T.astype(np.float32)
    kv2lT = inp["L0_attn_kv2l_w"].T.astype(np.float32)
    def col_swap(m):
        s = m.copy()
        s[:, 0::2], s[:, 1::2] = m[:, 1::2], m[:, 0::2]
        return s
    def pack2(wt):
        m = np.zeros((128, 64), dtype=np.float32)
        m[0:64, 0:32] = wt
        m[64:128, 32:64] = wt
        return m
    d["wlat_q"] = np.concatenate([pack2(q2lT), pack2(col_swap(q2lT))], axis=1)
    d["wlat_kv"] = np.concatenate([pack2(kv2lT), pack2(col_swap(kv2lT))], axis=1)
    d["wlat_v"] = pack2(kv2lT)
    def swap_b(b):
        s = b.copy()
        s[0::2], s[1::2] = b[1::2], b[0::2]
        return s
    blat = np.zeros((128, 4), dtype=np.float32)
    qb, kb = inp["L0_attn_q2l_b"], inp["L0_attn_kv2l_b"]
    blat[0:32, 0] = qb; blat[32:64, 0] = qb
    blat[64:96, 0] = swap_b(qb); blat[96:128, 0] = swap_b(qb)
    blat[0:32, 1] = kb; blat[32:64, 1] = kb
    blat[64:96, 1] = swap_b(kb); blat[96:128, 1] = swap_b(kb)
    blat[0:32, 2] = kb; blat[32:64, 2] = kb
    d["blat"] = blat
    wqk = np.zeros((64, 64), dtype=np.float32)
    wqk[0:32, 0:32] = inp["L0_attn_wqk_w"].T
    wqk[32:64, 32:64] = inp["L0_attn_wqk_w"].T
    d["wwqk"] = wqk
    bwqk = np.zeros((64, 1), dtype=np.float32)
    bwqk[0:32, 0] = inp["L0_attn_wqk_b"]
    bwqk[32:64, 0] = inp["L0_attn_wqk_b"]
    d["bwqk"] = bwqk
    l2o = np.zeros((64, 128), dtype=np.float32)
    l2o[0:32, 0:64] = inp["L0_attn_l2o_w"].T
    l2o[32:64, 64:128] = inp["L0_attn_l2o_w"].T
    d["wl2o"] = l2o
    d["bl2o"] = np.concatenate([inp["L0_attn_l2o_b"],
                                inp["L0_attn_l2o_b"]]).reshape(128, 1).astype(np.float32)

    d["ident"] = np.eye(128, dtype=np.float32)
    kk = np.arange(128).reshape(128, 1)
    qq = np.arange(128).reshape(1, 128)
    d["mask01"] = (kk <= qq).astype(np.float32)
    cosT, sinpT = _rope_tables()
    d["costab"] = np.concatenate([cosT, cosT], axis=0)
    d["sintab"] = np.concatenate([sinpT, sinpT], axis=0)
    onesA = np.zeros((1, 128), dtype=np.float32); onesA[0, 0:64] = 1.0
    onesB = np.zeros((1, 128), dtype=np.float32); onesB[0, 64:128] = 1.0
    d["onesA"] = onesA
    d["onesB"] = onesB
    for k in d:
        if d[k].dtype == np.float64:
            d[k] = d[k].astype(np.float32)
    return d


_SHARED_SPECS = [
    ("wqkv", [NFT, 128, 1024], F32), ("bqkv", [128, NFT], F32),
    ("wcproj", [NCT, 128, 1024], F32), ("bcproj", [128, NCT], F32),
    ("wg", [NFT, 128, 1024], I32), ("wu", [NFT, 128, 1024], I32),
    ("wd", [NCT, 128, 3072], I32),
    ("wlat_q", [128, 128], F32), ("wlat_kv", [128, 128], F32),
    ("wlat_v", [128, 64], F32),
    ("blat", [128, 4], F32), ("wwqk", [64, 64], F32), ("bwqk", [64, 1], F32),
    ("wl2o", [64, 128], F32), ("bl2o", [128, 1], F32),
    ("ident", [128, 128], F32), ("mask01", [128, 128], F32),
    ("costab", [64, T], F32), ("sintab", [64, T], F32),
    ("onesA", [1, 128], F32), ("onesB", [1, 128], F32),
]



def build_nc(taps=(), upto="p5", ngroups=NG, p1stage="full", gmode="full"):
    nc = bass.Bass("TRN2", target_bir_lowering=False, debug=False)
    taps = set(taps)

    dram = {}
    dram["x"] = nc.dram_tensor("x", [T, C], F32, kind="ExternalInput").ap()
    dram["xT"] = nc.dram_tensor("xT", [C, T], F32, kind="ExternalInput").ap()
    for nm, shp, dt in _SHARED_SPECS:
        dram[nm] = nc.dram_tensor(nm, shp, dt, kind="ExternalInput").ap()
    out_h = nc.dram_tensor("out", [C, T], F32, kind="ExternalOutput").ap()
    dbg = {}
    def tapout(name, shape, dt):
        if name in taps:
            if dt == BF16:
                dbg[name] = nc.dram_tensor(
                    "dbg_" + name, [shape[0], shape[1] // 2], F32,
                    kind="ExternalOutput").ap().bitcast(BF16)
            else:
                dbg[name] = nc.dram_tensor("dbg_" + name, shape, dt,
                                           kind="ExternalOutput").ap()

    tapout("xnT", [128, NCT * T], BF16)
    tapout("qkvt", [128, 3 * T], BF16)
    tapout("aug_q", [64, T], BF16)
    tapout("aug_k", [64, T], BF16)
    tapout("aug_v", [128, 16 * 66], BF16)
    tapout("ylr", [64, 512], BF16)
    tapout("latq0", [128, 512], F32)
    tapout("latk0", [128, 512], F32)
    tapout("ropeq0", [64, 512], BF16)
    tapout("vbuf0", [64, 512], BF16)
    tapout("qkvt_end", [128, T], BF16)
    tapout("rcp", [1, 1024], F32)
    tapout("yT", [128, NG * T], BF16)
    tapout("s_b", [128, NCT * T], BF16)
    tapout("h", [128, NFT * T], BF16)

    em = EM()
    em.dma_queue("dL", "sync")
    em.dma_queue("dG", "gpsimd")
    em.sem("pe", "tensor")
    em.sem("gp", "gpsimd")
    em.sem("dv", "vector")
    em.sem("sc", "scalar")

    es = ExitStack()
    with es:
        sb = {}
        def alloc(name, shape, dt):
            t = es.enter_context(nc.sbuf_tensor(name, shape, dt))
            sb[name] = t
            return t

        ident_b = alloc("ident_b", [128, 128], BF16)
        mask01_b = alloc("mask01_b", [128, 128], BF16)
        wlat_q_b = alloc("wlat_q_b", [128, 128], BF16)
        wlat_kv_b = alloc("wlat_kv_b", [128, 128], BF16)
        wlat_v_b = alloc("wlat_v_b", [128, 64], BF16)
        wwqk_b = alloc("wwqk_b", [64, 64], BF16)
        wl2o_b = alloc("wl2o_b", [64, 128], BF16)
        onesA_f = alloc("onesA_f", [1, 128], F32)
        onesB_f = alloc("onesB_f", [1, 128], F32)
        bqkv_f = alloc("bqkv_f", [128, NFT], F32)
        blat_f = alloc("blat_f", [128, 4], F32)
        bwqk_f = alloc("bwqk_f", [64, 1], F32)
        bl2o_f = alloc("bl2o_f", [128, 1], F32)
        bcproj_f = alloc("bcproj_f", [128, NCT], F32)

        xnT = alloc("xnT", [128, NCT * T], BF16)
        s_b = alloc("s_b", [128, NCT * T], BF16)
        arena1 = alloc("arena1", [128, NFT * T], BF16)
        arena2 = alloc("arena2", [128, 8 * T], BF16)
        wbuf_i = alloc("wbuf_i", [128, 1024], I32)

        wbuf_b = alloc("wbuf_b", [128, 2048], BF16)

        rtmp0 = alloc("rtmp0", [128, 512], F32)
        rtmp1 = alloc("rtmp1", [128, 512], F32)
        xn_bufs = [rtmp0[:].bitcast(BF16), rtmp1[:].bitcast(BF16)]
        st_f = alloc("st_f", [128, 160], F32)

        a1 = arena1[:]
        off = [0]
        def carve(cols):
            o = off[0]
            off[0] += cols
            assert off[0] <= NFT * T, off[0]
            return a1[:, o:o + cols]
        cstab = carve(4096).bitcast(F32)
        cos_f = cstab[0:64, :]
        sin_f = cstab[64:128, :]
        qkvt = [{k: carve(T) for k in "qkv"} for _ in range(2)]
        aug_q = [carve(T) for _ in range(2)]
        aug_k = [carve(T) for _ in range(2)]
        aug_v = [carve(16 * 66) for _ in range(2)]
        latq = [carve(1024).bitcast(F32) for _ in range(2)]
        latk = [carve(1024).bitcast(F32) for _ in range(2)]
        rvb = [carve(512) for _ in range(4)]
        ropeq = [rvb[ch][0:64, :] for ch in range(4)]
        vbuf = [rvb[ch][64:128, :] for ch in range(4)]
        ptA_pair = [carve(1024) for _ in range(2)]
        ptB_pair = [carve(1024) for _ in range(2)]
        ylr = carve(512)
        rsc = carve(2048).bitcast(F32)
        rsrow = rsc[0:1, :]
        rlrow = rsc[32:33, :]
        rb_f = carve(1024).bitcast(F32)
        ytmp_f = carve(1024).bitcast(F32)
        wf0 = carve(2048).bitcast(F32)
        wf1 = carve(2048).bitcast(F32)
        wpb0 = carve(1024)
        wpb1 = carve(1024)
        yT = arena2[:, 0:NG * T]
        gate_b = arena2[:, 0:T]
        up_b = arena2[:, T:2 * T]
        wd_i = arena2[:, 2 * T:2 * T + 6144].bitcast(I32)
        wbuf_i2 = arena2[:, 2 * T:2 * T + 2048].bitcast(I32)
        wd_b = arena2[:, 2 * T + 6144:2 * T + 9216]
        gbuf_b = arena2[:, 2 * T + 9216:2 * T + 9728]
        tbuf_b = arena2[:, 2 * T + 9728:2 * T + 10240]
        xnT_f = xnT[:].bitcast(F32)
        xt_p = [xnT_f[:, 0:2048], xnT_f[:, 2048:4096]]
        ob_p = xt_p
        wd_b2 = xnT[:, 8192:11264]

        PA = es.enter_context(nc.psum_tensor("PA", [128, 512], F32))[:]
        PD = es.enter_context(nc.psum_tensor("PD", [128, 512], F32))[:]
        SA01 = es.enter_context(nc.psum_tensor("SA01", [128, 1024], F32))[:]
        SB01 = es.enter_context(nc.psum_tensor("SB01", [128, 1024], F32))[:]
        SA0 = SA01[:, 0:512]
        SA1 = SA01[:, 512:1024]
        SBo = SB01[:, 0:512]
        PB = SB01[:, 512:1024]
        PG = es.enter_context(nc.psum_tensor("PG", [128, 512], F32))[:]
        PT = es.enter_context(nc.psum_tensor("PT", [128, 1024], BF16))[:]

        sems = {}
        for snm in list(em.cnt):
            sems[snm] = es.enter_context(nc.semaphore(snm))

        R = {}
        def res(name):
            if name not in R:
                R[name] = Res(name, exclusive=name in
                              ("PA", "PB", "PD", "SA0", "SA1", "SBo", "PG", "PT"))
            return R[name]
        rPA, rPB, rPD, rSA0, rSA1, rSBo, rPG, rPT = (res(n) for n in
            ("PA", "PB", "PD", "SA0", "SA1", "SBo", "PG", "PT"))

        def alias_seed(dst, *srcs):
            for s in srcs:
                if s.w is not None:
                    dst.rs.append((s.w, "@alias"))
                dst.rs.extend(s.rs)

        E = em.emit

        def ld(dst, src, rdst):
            return E("sync", lambda e, d=dst, s=src: e.dma_start(out=d, in_=s),
                     writes=[rdst], inc="dL", amt=16)

        def cast(dst, src, rd, wr):
            return E("vector", lambda e, d=dst, s=src: e.tensor_copy(d, s),
                     reads=[rd], writes=[wr], inc="dv")

        ld(wf0[:, 0:128], dram["ident"], res("wf0"))
        cast(ident_b[:], wf0[:, 0:128], res("wf0"), res("ident_b"))
        ld(wf0[:, 0:128], dram["mask01"], res("wf0"))
        cast(mask01_b[:], wf0[:, 0:128], res("wf0"), res("mask01_b"))
        ld(wf0[:, 0:128], dram["wlat_q"], res("wf0"))
        cast(wlat_q_b[:], wf0[:, 0:128], res("wf0"), res("wlat_q_b"))
        ld(wf1[:, 0:128], dram["wlat_kv"], res("wf1"))
        cast(wlat_kv_b[:], wf1[:, 0:128], res("wf1"), res("wlat_kv_b"))
        ld(wf0[:, 0:64], dram["wlat_v"], res("wf0"))
        cast(wlat_v_b[:], wf0[0:128, 0:64], res("wf0"), res("wlat_v_b"))
        ld(wf1[0:64, 0:64], dram["wwqk"], res("wf1"))
        cast(wwqk_b[:], wf1[0:64, 0:64], res("wf1"), res("wwqk_b"))
        ld(wf0[0:64, 0:128], dram["wl2o"], res("wf0"))
        cast(wl2o_b[:], wf0[0:64, 0:128], res("wf0"), res("wl2o_b"))
        ld(onesA_f[:], dram["onesA"], res("onesA"))
        ld(onesB_f[:], dram["onesB"], res("onesB"))
        ld(bqkv_f[:], dram["bqkv"], res("bqkv"))
        ld(blat_f[:], dram["blat"], res("blat"))
        ld(bwqk_f[:], dram["bwqk"], res("bwqk"))
        ld(bl2o_f[:], dram["bl2o"], res("bl2o"))
        ld(bcproj_f[:], dram["bcproj"], res("bcproj"))
        ld(cos_f, dram["costab"], res("cos"))
        ld(sin_f, dram["sintab"], res("sin"))
        for _bk, _rbk in ((SA0, rSA0), (SA1, rSA1), (SBo, rSBo), (PB, rPB)):
            E("tensor", lambda e, _bk=_bk: e.matmul(
                _bk, onesA_f[:], cos_f[0:1, 0:512], start=True, stop=True,
                skip_group_check=True),
              reads=[res("onesA"), res("cos")], writes=[_rbk], inc="pe")
        for pp in range(2):
            rav0 = res("aug_v%d" % pp)
            E("vector", lambda e, pp=pp: e.memset(aug_v[pp], 0.0),
              writes=[rav0], inc="dv")
            for col in (32, 65):
                ap = aug_v[pp].rearrange("p (k m) -> p k m", k=16)[:, :, col:col + 1]
                E("vector", lambda e, ap=ap: e.memset(ap, 1.0),
                  writes=[rav0], inc="dv")

        rxnT = res("xnT")
        epscol = st_f[:, 159:160]
        E("vector", lambda e: e.memset(epscol, EPS), writes=[res("eps")], inc="dv")
        for tt in range(NT):
            bi = tt % 2
            wfp = wf0 if bi == 0 else wf1
            rwf = res("wf0" if bi == 0 else "wf1")
            xn_b = xn_bufs[tt % 2]
            jk_b = xn_b
            rxn = res("xn_b%d" % (tt % 2))
            st = st_f[:, tt * 8:tt * 8 + 8]
            rst = res("st%d" % tt)
            E("sync", lambda e, d=wfp, s=dram["x"][tt * 128:(tt + 1) * 128, :]:
              e.dma_start(out=d, in_=s), writes=[rwf], inc="dL", amt=16)
            E("vector", lambda e, s=wfp, st=st: e.reduce_sum(st[:, 0:1], s, axis=AX.X),
              reads=[rwf], writes=[rst], inc="dv")
            E("scalar", lambda e, s=wfp, st=st, jk_b=jk_b: e.activation(
                jk_b, s, AF.Square, accum_out=st[:, 1:2]),
              reads=[rwf], writes=[rxn, rst], inc="sc")
            E("vector", lambda e, st=st: e.tensor_scalar(
                st[:, 2:3], st[:, 0:1], st[:, 0:1], 1.0 / (C * C), ALU.mult, ALU.mult),
              reads=[rst], writes=[rst], inc="dv")
            E("vector", lambda e, st=st: e.tensor_scalar(
                st[:, 3:4], st[:, 1:2], 1.0 / C, st[:, 2:3], ALU.mult, ALU.subtract),
              reads=[rst], writes=[rst], inc="dv")
            E("scalar", lambda e, st=st: e.activation(st[:, 4:5], st[:, 3:4], AF.Ln,
                                                      bias=epscol),
              reads=[rst, res("eps")], writes=[rst], inc="sc")
            E("scalar", lambda e, st=st: e.activation(st[:, 5:6], st[:, 4:5], AF.Exp,
                                                      scale=-0.5),
              reads=[rst], writes=[rst], inc="sc")
            E("vector", lambda e, st=st: e.tensor_scalar(
                st[:, 6:7], st[:, 0:1], st[:, 5:6], 1.0 / C, ALU.mult, ALU.mult),
              reads=[rst], writes=[rst], inc="dv")
            E("vector", lambda e, s=wfp, st=st, xn_b=xn_b: e.tensor_scalar(
                xn_b[:], s, st[:, 5:6], st[:, 6:7], ALU.mult, ALU.subtract),
              reads=[rwf, rst], writes=[rxn], inc="dv")
            for ct in range(NCT):
                E("tensor", lambda e, ct=ct, xn_b=xn_b: e.transpose(
                    PT[:, ct * 128:(ct + 1) * 128], xn_b[:, ct * 128:(ct + 1) * 128],
                    ident_b[:]),
                  reads=[rxn, res("ident_b")], writes=[rPT], inc="pe")
            xr = xnT[:].rearrange("p (c t) -> p c t", c=NCT)[:, :, tt * 128:(tt + 1) * 128]
            E("scalar", lambda e, xr=xr: e.copy(xr, PT[:, 0:1024].rearrange(
                "p (c t) -> p c t", c=NCT)),
              reads=[rPT], writes=[rxnT, res("xnT%d" % (tt // 4))], inc="sc")
        if "xnT" in dbg:
            E("gpsimd", lambda e: e.dma_start(out=dbg["xnT"], in_=xnT[:]),
              reads=[rxnT], inc="dG", amt=16)

        alias_seed(res("rtmp0"), res("xn_b0"))
        alias_seed(res("rtmp1"), res("xn_b1"))
        ryT = res("yT")
        pending = [None]
        pending_sc = []

        def emit_qkv_prep(g):
            p = g % 2
            rq = res("qkvt_q%d" % p)
            rk = res("qkvt_k%d" % p)
            rv = res("qkvt_v%d" % p)
            raq = res("aug_q%d" % p)
            rak = res("aug_k%d" % p)
            rav = res("aug_v%d" % p)

            if gmode == "none" and g > 0:
                return
            for kind, (ofi, qt_dst, rdst) in enumerate(
                    ((g, qkvt[p]["q"], rq), (8 + g, qkvt[p]["k"], rk),
                     (16 + g, qkvt[p]["v"], rv))):
                wfp = wf0 if kind % 2 == 0 else wf1
                rwf = res("wf0" if kind % 2 == 0 else "wf1")
                wpb = wpb0 if kind % 2 == 0 else wpb1
                rwpb = res("wpb%d" % (kind % 2))
                E("sync", lambda e, d=wfp, s=dram["wqkv"][ofi]:
                  e.dma_start(out=d, in_=s), writes=[rwf], inc="dL", amt=16)
                E("vector", lambda e, d=wpb, s=wfp: e.tensor_copy(d, s),
                  reads=[rwf], writes=[rwpb], inc="dv")
                if gmode == "dma" and g > 0:
                    continue
                for ch in range(4):
                    ps, rps = (PA, rPA) if ch % 2 == 0 else (PB, rPB)
                    for c in range(NCT):
                        E("tensor", lambda e, ps=ps, c=c, ch=ch, wpb=wpb: e.matmul(
                            ps[:], wpb[:, c * 128:(c + 1) * 128],
                            xnT[:, c * T + ch * 512: c * T + ch * 512 + 512],
                            start=(c == 0), stop=(c == NCT - 1),
                            skip_group_check=True),
                          reads=[rwpb, res("xnT%d" % ch)], writes=[rps], inc="pe")
                    if gmode == "mm" and g > 0:
                        continue
                    E("vector", lambda e, ps=ps, d=qt_dst, ch=ch, ofi=ofi:
                      e.tensor_scalar(d[:, ch * 512:(ch + 1) * 512], ps[:],
                                      bqkv_f[:, ofi:ofi + 1], None, ALU.add),
                      reads=[rps, res("bqkv")], writes=[rdst], inc="dv")
            if "qkvt" in dbg and g == 0:
                for i, knm in enumerate("qkv"):
                    E("gpsimd", lambda e, src=qkvt[p][knm], i=i:
                      e.dma_start(out=dbg["qkvt"][:, i * T:(i + 1) * T], in_=src),
                      reads=[(rq, rk, rv)[i]], inc="dG", amt=16)

            if p1stage < "b" or (gmode == "qkv" and g > 0):
                return
            for ch in range(4):
                cp = ch % 2
                sl = slice(ch * 512, (ch + 1) * 512)
                lq, rlq = latq[cp], res("latq%d" % cp)
                lk, rlk = latk[cp], res("latk%d" % cp)
                rp_, rrp = ropeq[ch], res("ropeq%d" % ch)
                vb, rvb_ = vbuf[ch], res("vbuf%d" % ch)
                rt0, rt1 = res("rtmp0"), res("rtmp1")
                QL, rQL = (PA, rPA) if cp == 0 else (PB, rPB)
                KL, rKL = (PG, rPG) if cp == 0 else (PD, rPD)
                VL, rVL = (SA0, rSA0) if cp == 0 else (SA1, rSA1)
                E("tensor", lambda e, sl=sl, p=p, QL=QL: e.matmul(
                    QL[:], wlat_q_b[:], qkvt[p]["q"][:, sl], start=True, stop=True,
                    skip_group_check=True),
                  reads=[res("wlat_q_b"), rq], writes=[rQL], inc="pe")
                E("tensor", lambda e, sl=sl, p=p, KL=KL: e.matmul(
                    KL[:], wlat_kv_b[:], qkvt[p]["k"][:, sl], start=True, stop=True,
                    skip_group_check=True),
                  reads=[res("wlat_kv_b"), rk], writes=[rKL], inc="pe")
                E("tensor", lambda e, sl=sl, p=p, VL=VL: e.matmul(
                    VL[0:64, :], wlat_v_b[:], qkvt[p]["v"][:, sl], start=True,
                    stop=True, skip_group_check=True),
                  reads=[res("wlat_v_b"), rv], writes=[rVL], inc="pe")
                E("scalar", lambda e, lq=lq, QL=QL: e.activation(
                    lq[:, 0:512], QL[:], AF.Identity, bias=blat_f[:, 0:1]),
                  reads=[rQL, res("blat")], writes=[rlq], inc="sc")
                E("vector", lambda e, sl=sl, lq=lq: e.tensor_mul(
                    rtmp0[0:64, :], lq[0:64, 0:512], cos_f[:, sl]),
                  reads=[rlq, res("cos")], writes=[rt0], inc="dv")
                E("vector", lambda e, sl=sl, lq=lq: e.tensor_mul(
                    rtmp1[0:64, :], lq[64:128, 0:512], sin_f[:, sl]),
                  reads=[rlq, res("sin")], writes=[rt1], inc="dv")
                E("vector", lambda e, rp_=rp_: e.tensor_add(
                    rp_[:, 0:512], rtmp0[0:64, :], rtmp1[0:64, :]),
                  reads=[rt0, rt1], writes=[rrp], inc="dv")
                E("scalar", lambda e, lk=lk, KL=KL: e.activation(
                    lk[:, 0:512], KL[:], AF.Identity, bias=blat_f[:, 1:2]),
                  reads=[rKL, res("blat")], writes=[rlk], inc="sc")
                E("vector", lambda e, sl=sl, lk=lk: e.tensor_mul(
                    rtmp0[0:64, :], lk[0:64, 0:512], cos_f[:, sl]),
                  reads=[rlk, res("cos")], writes=[rt0], inc="dv")
                E("vector", lambda e, sl=sl, lk=lk: e.tensor_mul(
                    rtmp1[0:64, :], lk[64:128, 0:512], sin_f[:, sl]),
                  reads=[rlk, res("sin")], writes=[rt1], inc="dv")
                E("vector", lambda e, sl=sl, p=p: e.tensor_add(
                    aug_k[p][0:64, sl], rtmp0[0:64, :], rtmp1[0:64, :]),
                  reads=[rt0, rt1], writes=[rak], inc="dv")
                E("scalar", lambda e, vb=vb, VL=VL: e.activation(
                    vb[:, 0:512], VL[0:64, :], AF.Identity,
                    bias=blat_f[0:64, 2:3]),
                  reads=[rVL, res("blat")], writes=[rvb_], inc="sc")

                if "latq0" in dbg and g == 0 and ch == 0:
                    E("gpsimd", lambda e, lq=lq: e.dma_start(out=dbg["latq0"], in_=lq[:, 0:512]),
                      reads=[rlq], inc="dG", amt=16)
                if "latk0" in dbg and g == 0 and ch == 0:
                    E("gpsimd", lambda e, lk=lk: e.dma_start(out=dbg["latk0"], in_=lk[:, 0:512]),
                      reads=[rlk], inc="dG", amt=16)
                if "ropeq0" in dbg and g == 0 and ch == 0:
                    E("gpsimd", lambda e, rp_=rp_: e.dma_start(out=dbg["ropeq0"], in_=rp_[:, 0:512]),
                      reads=[rrp], inc="dG", amt=16)
                if "vbuf0" in dbg and g == 0 and ch == 0:
                    E("gpsimd", lambda e, vb=vb: e.dma_start(out=dbg["vbuf0"], in_=vb[:, 0:512]),
                      reads=[rvb_], inc="dG", amt=16)

        def emit_prep_b(g):
            p = g % 2
            raq = res("aug_q%d" % p)
            rav = res("aug_v%d" % p)
            wbank = ((PA, rPA), (PB, rPB), (PG, rPG), (PD, rPD))
            for ch in range(4):
                rp_, rrp = ropeq[ch], res("ropeq%d" % ch)
                WB, rWB = wbank[ch]
                E("tensor", lambda e, rp_=rp_, WB=WB: e.matmul(
                    WB[0:64, :], wwqk_b[:], rp_[:, 0:512],
                    start=True, stop=True, skip_group_check=True),
                  reads=[res("wwqk_b"), rrp], writes=[rWB], inc="pe")
            for ch in range(4):
                vb, rvb_ = vbuf[ch], res("vbuf%d" % ch)
                for kt in range(4):
                    E("tensor", lambda e, kt=kt, vb=vb, ch=ch: e.transpose(
                        PT[:, ch * 256 + kt * 64: ch * 256 + (kt + 1) * 64],
                        vb[:, kt * 128:(kt + 1) * 128], ident_b[64:128, 64:128]),
                      reads=[rvb_, res("ident_b")], writes=[rPT], inc="pe")
            for ch in range(4):
                sl = slice(ch * 512, (ch + 1) * 512)
                WB, rWB = wbank[ch]
                E("scalar", lambda e, sl=sl, p=p, WB=WB: e.activation(
                    aug_q[p][0:64, sl], WB[0:64, :], AF.Identity, bias=bwqk_f[:]),
                  reads=[rWB, res("bwqk")], writes=[raq], inc="sc")
                srcT = PT[:, ch * 256:ch * 256 + 256].rearrange(
                    "p (k m) -> p k m", k=4)
                dstA = aug_v[p].rearrange("p (k m) -> p k m", k=16)[
                    :, ch * 4:(ch + 1) * 4, 0:32]
                dstB = aug_v[p].rearrange("p (k m) -> p k m", k=16)[
                    :, ch * 4:(ch + 1) * 4, 33:65]
                E("vector", lambda e, d=dstA, s=srcT[:, :, 0:32]: e.tensor_copy(d, s),
                  reads=[rPT], writes=[rav], inc="dv")
                E("vector", lambda e, d=dstB, s=srcT[:, :, 32:64]: e.tensor_copy(d, s),
                  reads=[rPT], writes=[rav], inc="dv")
            if "aug_q" in dbg and g == 0:
                E("gpsimd", lambda e, p=p: e.dma_start(out=dbg["aug_q"], in_=aug_q[p][0:64, :]),
                  reads=[raq], inc="dG", amt=16)
            if "aug_k" in dbg and g == 0:
                E("gpsimd", lambda e, p=p: e.dma_start(out=dbg["aug_k"], in_=aug_k[p][0:64, :]),
                  reads=[res("aug_k%d" % p)], inc="dG", amt=16)
            if "aug_v" in dbg and g == 0:
                E("gpsimd", lambda e, p=p: e.dma_start(out=dbg["aug_v"], in_=aug_v[p]),
                  reads=[rav], inc="dG", amt=16)

        def emit_pass3(g):
            if p1stage < "c":
                return
            p = g % 2
            raq = res("aug_q%d" % p)
            rak = res("aug_k%d" % p)
            rav = res("aug_v%d" % p)
            for j in range(4):
                ni = 4 * j + 4
                avq = []
                for i in range(ni):
                    qoff = max(512 * j, 128 * i)
                    N = 512 * j + 512 - qoff
                    pp = (i // 2) % 2
                    off = (i % 2) * 512
                    rpa, rpb = res("pt_A%d" % pp), res("pt_B%d" % pp)
                    E("tensor", lambda e, i=i, qoff=qoff, N=N, p=p, off=off:
                      e.matmul(
                        SA01[:, off:off + N], aug_k[p][0:32, i * 128:(i + 1) * 128],
                        aug_q[p][0:32, qoff:qoff + N],
                        start=True, stop=True, skip_group_check=True),
                      reads=[rak, raq],
                      writes=[rSA0 if off == 0 else rSA1], inc="pe")
                    E("tensor", lambda e, i=i, qoff=qoff, N=N, p=p, off=off:
                      e.matmul(
                        SB01[:, off:off + N], aug_k[p][32:64, i * 128:(i + 1) * 128],
                        aug_q[p][32:64, qoff:qoff + N],
                        start=True, stop=True, skip_group_check=True),
                      reads=[rak, raq],
                      writes=[rSBo if off == 0 else rPB], inc="pe")
                    if i % 2 == 1:
                        NE = 512 + N
                        E("scalar", lambda e, NE=NE, pp=pp: e.activation(
                            ptA_pair[pp][:, 0:NE], SA01[:, 0:NE], AF.Exp,
                            scale=SCL),
                          reads=[rSA0, rSA1], writes=[rpa], inc="sc")
                        E("scalar", lambda e, NE=NE, pp=pp: e.activation(
                            ptB_pair[pp][:, 0:NE], SB01[:, 0:NE], AF.Exp,
                            scale=SCL),
                          reads=[rSBo, rPB], writes=[rpb], inc="sc")
                        for ii in (i - 1, i):
                            if max(512 * j, 128 * ii) == 128 * ii:
                                o2 = (ii % 2) * 512
                                E("vector", lambda e, pp=pp, o2=o2: e.tensor_mul(
                                    ptA_pair[pp][:, o2:o2 + 128],
                                    ptA_pair[pp][:, o2:o2 + 128], mask01_b[:]),
                                  reads=[res("mask01_b"), rpa], writes=[rpa],
                                  inc="dv")
                                E("vector", lambda e, pp=pp, o2=o2: e.tensor_mul(
                                    ptB_pair[pp][:, o2:o2 + 128],
                                    ptB_pair[pp][:, o2:o2 + 128], mask01_b[:]),
                                  reads=[res("mask01_b"), rpb], writes=[rpb],
                                  inc="dv")
                    def em_av(i=i, j=j, qoff=qoff, N=N, pp=pp, p=p, off=off,
                              rpa=rpa, rpb=rpb):
                        E("tensor", lambda e: e.matmul(
                            PG[0:33, qoff - 512 * j:qoff - 512 * j + N],
                            aug_v[p][:, i * 66:i * 66 + 33],
                            ptA_pair[pp][:, off:off + N],
                            start=(i == 0), stop=(i == 4 * j + 3),
                            skip_group_check=True),
                          reads=[rav, rpa], writes=[rPG], inc="pe")
                        E("tensor", lambda e: e.matmul(
                            PD[0:33, qoff - 512 * j:qoff - 512 * j + N],
                            aug_v[p][:, i * 66 + 33:(i + 1) * 66],
                            ptB_pair[pp][:, off:off + N],
                            start=(i == 0), stop=(i == 4 * j + 3),
                            skip_group_check=True),
                          reads=[rav, rpb], writes=[rPD], inc="pe")
                    avq.append(em_av)
                    if i in (2, 3) and pending_sc:
                        pending_sc.pop(0)()
                    if i >= 3:
                        avq[i - 3]()
                for q_ in range(max(0, ni - 3), ni):
                    avq[q_]()
                if p1stage < "d":
                    continue
                if pending[0] is not None:
                    pending[0]()
                    pending[0] = None
                rrs, rrl = res("rsrow"), res("rlrow")
                rylr = res("ylr")
                E("vector", lambda e: e.tensor_copy(rsrow[:, 0:512], PG[32:33, :]),
                  reads=[rPG], writes=[rrs], inc="dv")
                E("vector", lambda e: e.tensor_copy(rsrow[:, 512:1024], PD[32:33, :]),
                  reads=[rPD], writes=[rrs], inc="dv")
                def em_ln(rrs=rrs, rrl=rrl):
                    E("scalar", lambda e: e.activation(rlrow[:, 0:1024],
                                                       rsrow[:, 0:1024], AF.Ln),
                      reads=[rrs], writes=[rrl], inc="sc")
                def em_exp(rrs=rrs, rrl=rrl):
                    E("scalar", lambda e: e.activation(rsrow[:, 0:1024],
                                                       rlrow[:, 0:1024], AF.Exp,
                                                       scale=-1.0),
                      reads=[rrl], writes=[rrs], inc="sc")
                pending_sc.clear()
                pending_sc.extend([em_ln, em_exp])
                if "rcp" in dbg and g == 0 and j == 0:
                    E("gpsimd", lambda e: e.dma_start(out=dbg["rcp"],
                                                      in_=rsrow[:, 0:1024]),
                      reads=[rrs], inc="dG", amt=16)
                E("vector", lambda e: e.tensor_copy(ylr[0:32, 0:512], PG[0:32, :]),
                  reads=[rPG], writes=[rylr], inc="dv")
                E("vector", lambda e: e.tensor_copy(ylr[32:64, 0:512], PD[0:32, :]),
                  reads=[rPD], writes=[rylr], inc="dv")
                if "ylr" in dbg and g == 0 and j == 0:
                    E("gpsimd", lambda e: e.dma_start(out=dbg["ylr"],
                                                      in_=ylr[0:64, 0:512]),
                      reads=[rylr], inc="dG", amt=16)
                def em_epi(g=g, j=j, rrs=rrs, rylr=rylr):
                    E("tensor", lambda e: e.matmul(
                        SA0[:], onesA_f[:], rsrow[:, 0:512],
                        start=True, stop=False, skip_group_check=True),
                      reads=[res("onesA"), rrs], writes=[rSA0], inc="pe")
                    E("tensor", lambda e: e.matmul(
                        SA0[:], onesB_f[:], rsrow[:, 512:1024],
                        start=False, stop=True, skip_group_check=True),
                      reads=[res("onesB"), rrs], writes=[rSA0], inc="pe")
                    rrb = res("rb_f")
                    E("vector", lambda e: e.tensor_copy(rb_f[:, 0:512], SA0[:]),
                      reads=[rSA0], writes=[rrb], inc="dv")
                    E("tensor", lambda e: e.matmul(
                        SA1[:], wl2o_b[:], ylr[0:64, 0:512], start=True, stop=True,
                        skip_group_check=True),
                      reads=[res("wl2o_b"), rylr], writes=[rSA1], inc="pe")
                    ryt = res("ytmp_f")
                    E("vector", lambda e: e.tensor_mul(ytmp_f[:, 0:512], SA1[:],
                                                       rb_f[:, 0:512]),
                      reads=[rSA1, rrb], writes=[ryt], inc="dv")
                    E("vector", lambda e: e.tensor_scalar(
                        yT[:, g * T + j * 512: g * T + (j + 1) * 512],
                        ytmp_f[:, 0:512], bl2o_f[:], None, ALU.add),
                      reads=[ryt, res("bl2o")], writes=[ryT], inc="dv")
                pending[0] = em_epi

        if upto >= "p1":
            for g in range(ngroups):
                emit_qkv_prep(g)
                if g >= 1:
                    emit_pass3(g - 1)
                emit_prep_b(g)
            emit_pass3(ngroups - 1)
        while pending_sc:
            pending_sc.pop(0)()
        if pending[0] is not None:
            pending[0]()
            pending[0] = None
        if "qkvt_end" in dbg:
            E("gpsimd", lambda e: e.dma_start(out=dbg["qkvt_end"], in_=qkvt[0]["q"]),
              reads=[res("qkvt_q0")], inc="dG", amt=16)
        if "yT" in dbg:
            E("gpsimd", lambda e: e.dma_start(out=dbg["yT"], in_=yT),
              reads=[ryT], inc="dG", amt=16)

        rs_b = res("s_b")
        for dtile in (range(NCT) if upto >= "p2" and ngroups == NG else ()):
            wfp = wf0 if dtile % 2 == 0 else wf1
            rwf = res("wf0" if dtile % 2 == 0 else "wf1")
            wpb = wpb0 if dtile % 2 == 0 else wpb1
            rwpb = res("wpb%d" % (dtile % 2))
            E("sync", lambda e, d=wfp, s=dram["wcproj"][dtile]:
              e.dma_start(out=d, in_=s), writes=[rwf], inc="dL", amt=16)
            E("vector", lambda e, d=wpb, s=wfp: e.tensor_copy(d, s),
              reads=[rwf], writes=[rwpb], inc="dv")
            for ch in range(4):
                ps, rps = (PA, rPA) if ch % 2 == 0 else (PB, rPB)
                for c in range(NCT):
                    E("tensor", lambda e, ps=ps, c=c, ch=ch, wpb=wpb: e.matmul(
                        ps[:], wpb[:, c * 128:(c + 1) * 128],
                        yT[:, c * T + ch * 512: c * T + ch * 512 + 512],
                        start=(c == 0), stop=(c == NCT - 1),
                        skip_group_check=True),
                      reads=[rwpb, ryT], writes=[rps], inc="pe")
                E("scalar", lambda e, ps=ps, ch=ch, dtile=dtile: e.activation(
                    s_b[:, dtile * T + ch * 512: dtile * T + (ch + 1) * 512],
                    ps[:], AF.Identity, bias=bcproj_f[:, dtile:dtile + 1]),
                  reads=[rps, res("bcproj")], writes=[rs_b], inc="sc")
        if "s_b" in dbg:
            E("gpsimd", lambda e: e.dma_start(out=dbg["s_b"], in_=s_b[:]),
              reads=[rs_b], inc="dG", amt=16)

        rh = res("h")
        rga, rup = res("gate_b"), res("up_b")
        alias_seed(rga, ryT)
        alias_seed(rup, ryT)
        alias_seed(res("wbuf_i1"), ryT)
        a1names = (["cos", "sin", "ylr", "rsrow", "rlrow", "rb_f", "ytmp_f",
                    "wf0", "wf1", "wpb0", "wpb1"]
                   + ["qkvt_%s%d" % (k, p) for k in "qkv" for p in range(2)]
                   + ["aug_%s%d" % (k, p) for k in "qkv" for p in range(2)]
                   + ["latq0", "latq1", "latk0", "latk1", "ropeq0", "ropeq1",
                      "vbuf0", "vbuf1", "pt_A0", "pt_A1", "pt_B0", "pt_B1"])
        alias_seed(rh, *[res(n) for n in a1names])
        def p4_dma(ht, which):
            wsrc = "wg" if which == 0 else "wu"
            wbi = wbuf_i if which == 0 else wbuf_i2
            rwi = res("wbuf_i%d" % which)
            E("sync", lambda e, s=dram[wsrc][ht], wbi=wbi: e.dma_start(out=wbi[:], in_=s),
              writes=[rwi], inc="dL", amt=16)
            half = which
            rwb_lo = res("wbuf_b%d_lo" % half)
            rwb_hi = res("wbuf_b%d_hi" % half)
            wslice = wbuf_b[:, half * 1024:(half + 1) * 1024]
            E("vector", lambda e, w=wslice, wbi=wbi: e.tensor_copy(
                w[:, 0:512], wbi[:, 0:512]),
              reads=[rwi], writes=[rwb_lo], inc="dv")
            E("vector", lambda e, w=wslice, wbi=wbi: e.tensor_copy(
                w[:, 512:1024], wbi[:, 512:1024]),
              reads=[rwi], writes=[rwb_hi], inc="dv")

        jobs = ([(ht, w) for ht in range(NFT) for w in (0, 1)]
                if upto >= "p4" else [])
        if jobs:
            p4_dma(*jobs[0])
            p4_dma(*jobs[1])
        bank4 = ((PA, rPA), (PB, rPB), (SA0, rSA0), (SA1, rSA1))
        for idx, (ht, which) in enumerate(jobs):
            half = which
            rwb_lo = res("wbuf_b%d_lo" % half)
            rwb_hi = res("wbuf_b%d_hi" % half)
            wslice = wbuf_b[:, half * 1024:(half + 1) * 1024]
            for ch in range(4):
                ps, rps = bank4[ch]
                for c in range(NCT):
                    E("tensor", lambda e, ps=ps, c=c, ch=ch, w=wslice: e.matmul(
                        ps[:], w[:, c * 128:(c + 1) * 128],
                        xnT[:, c * T + ch * 512: c * T + ch * 512 + 512],
                        start=(c == 0), stop=(c == NCT - 1),
                        skip_group_check=True),
                      reads=[rwb_lo if c < 4 else rwb_hi,
                             res("xnT%d" % ch)], writes=[rps], inc="pe")
            if idx + 2 < len(jobs):
                p4_dma(*jobs[idx + 2])
            for ch in range(4):
                ps, rps = bank4[ch]
                if which == 0:
                    E("scalar", lambda e, ps=ps, ch=ch: e.activation(
                        gate_b[:, ch * 512:(ch + 1) * 512], ps[:], AF.Silu,
                        scale=S127),
                      reads=[rps], writes=[rga], inc="sc")
                else:
                    E("scalar", lambda e, ps=ps, ch=ch: e.activation(
                        up_b[:, ch * 512:(ch + 1) * 512], ps[:], AF.Identity,
                        scale=S127),
                      reads=[rps], writes=[rup], inc="sc")
            if which == 1:
                for hh in range(2):
                    E("vector", lambda e, ht=ht, hh=hh: e.tensor_mul(
                        arena1[:, ht * T + hh * 1024: ht * T + (hh + 1) * 1024],
                        gate_b[:, hh * 1024:(hh + 1) * 1024],
                        up_b[:, hh * 1024:(hh + 1) * 1024]),
                      reads=[rga, rup], writes=[rh], inc="dv")
        if "h" in dbg:
            E("gpsimd", lambda e: e.dma_start(out=dbg["h"], in_=arena1[:]),
              reads=[rh], inc="dG", amt=16)

        final_toks = []
        rwdi = res("wd_i")
        alias_seed(rwdi, ryT, res("wbuf_i1"))
        alias_seed(res("gbuf_b"), ryT)
        alias_seed(res("wd_bp0"), ryT)
        alias_seed(res("tbuf_b"), ryT)
        for nm in ("xt0", "xt1", "wd_bp1"):
            alias_seed(res(nm), rxnT, *[res("xnT%d" % w) for w in range(4)])
        def p5_dma(dtile):
            E("sync", lambda e, s=dram["wd"][dtile]: e.dma_start(out=wd_i, in_=s),
              writes=[rwdi], inc="dL", amt=16)
            wdb = wd_b if dtile % 2 == 0 else wd_b2
            rwdb = res("wd_bp%d" % (dtile % 2))
            E("vector", lambda e, wdb=wdb: e.tensor_copy(wdb, wd_i),
              reads=[rwdi], writes=[rwdb], inc="dv")

        if upto >= "p5":
            p5_dma(0)
            p5_dma(1)
        for dtile in (range(NCT) if upto >= "p5" else ()):
            wdb = wd_b if dtile % 2 == 0 else wd_b2
            rwdb = res("wd_bp%d" % (dtile % 2))
            par = dtile % 2
            rxt = res("xt%d" % par)
            E("sync", lambda e, d=xt_p[par], s=dram["xT"][dtile * 128:(dtile + 1) * 128, :]:
              e.dma_start(out=d, in_=s), writes=[rxt], inc="dL", amt=16)
            rob = rxt
            bank4 = ((PA, rPA), (PB, rPB), (SA0, rSA0), (SA1, rSA1))
            for ch in range(4):
                ps, rps = bank4[ch]
                for k in range(NFT):
                    E("tensor", lambda e, ps=ps, k=k, ch=ch, wdb=wdb: e.matmul(
                        ps[:], wdb[:, k * 128:(k + 1) * 128],
                        arena1[:, k * T + ch * 512: k * T + ch * 512 + 512],
                        start=(k == 0), stop=(k == NFT - 1),
                        skip_group_check=True),
                      reads=[rwdb, rh], writes=[rps], inc="pe")
            if dtile + 2 < NCT:
                p5_dma(dtile + 2)
            for ch in range(4):
                ps, rps = bank4[ch]
                rgb, rtb = res("gbuf_b"), res("tbuf_b")
                E("scalar", lambda e, ps=ps: e.activation(
                    gbuf_b, ps[:], AF.Silu, scale=S127),
                  reads=[rps], writes=[rgb], inc="sc")
                E("vector", lambda e, ch=ch, dtile=dtile: e.tensor_mul(
                    tbuf_b, gbuf_b,
                    s_b[:, dtile * T + ch * 512: dtile * T + (ch + 1) * 512]),
                  reads=[rgb, rs_b], writes=[rtb], inc="dv")
                E("vector", lambda e, ch=ch, par=par: e.tensor_add(
                    ob_p[par][:, ch * 512:(ch + 1) * 512],
                    xt_p[par][:, ch * 512:(ch + 1) * 512], tbuf_b),
                  reads=[rtb, rxt], writes=[rob], inc="dv")
            tk = E("sync", lambda e, par=par, dtile=dtile: e.dma_start(
                out=out_h[dtile * 128:(dtile + 1) * 128, :], in_=ob_p[par]),
              reads=[rob], inc="dL", amt=16)
            final_toks.append(tk)

        def _final_wait(e):
            ins = None
            for pn, cnt in em.queue_totals("dL"):
                if cnt:
                    ins = e.wait_ge(sems[pn], cnt)
            for pn, cnt in em.queue_totals("dG"):
                if cnt:
                    ins = e.wait_ge(sems[pn], cnt)
            return ins
        E("gpsimd", _final_wait, waits=final_toks)

        with nc.Block() as block:
            em.replay(block, sems)
    return nc, dbg



def kernel(**inputs):
    shared = _host_prep(inputs)
    x = np.asarray(inputs["L0_attn_x"])
    nc, _ = build_nc()
    in_maps = []
    for c in range(NCORES):
        m = dict(shared)
        xs = np.ascontiguousarray(x[c])
        m["x"] = xs
        m["xT"] = np.ascontiguousarray(xs.T)
        in_maps.append(m)
    res = run_bass_kernel_spmd(nc, in_maps, core_ids=list(range(NCORES)))
    out = np.stack([np.ascontiguousarray(np.asarray(r["out"]).T)
                    for r in res.results], axis=0)
    return out


# version 28 (final)
# speedup vs baseline: 1.3101x; 1.3101x over previous
import math
import numpy as np
import ml_dtypes
import concourse.bass as bass
import concourse.mybir as mybir
from concourse.bass_utils import run_bass_kernel_spmd
from contextlib import ExitStack

F32 = mybir.dt.float32
BF16 = mybir.dt.bfloat16
I32 = mybir.dt.int32
AF = mybir.ActivationFunctionType
ALU = mybir.AluOpType
AX = mybir.AxisListType

T = 2048
C = 1024
H = 16
HD = 64
L = 32
FF = 3072
NG = 8
NT = T // 128
NCT = C // 128
NFT = FF // 128
S127 = 1.0 / 127.0
SCL = 1.0 / math.sqrt(L)
EPS = 1e-5
NCORES = 8

ENGS = ("sync", "tensor", "vector", "scalar", "gpsimd")


class Res:
    __slots__ = ("name", "w", "rs", "exclusive")
    def __init__(self, name, exclusive=False):
        self.name = name
        self.w = None
        self.rs = []
        self.exclusive = exclusive


class EM:
    def __init__(self):
        self.ev = {e: [] for e in ENGS}
        self.cnt = {}
        self.owner = {}
        self.queues = {}

    def sem(self, name, owner):
        self.cnt[name] = 0
        self.owner[name] = owner

    def dma_queue(self, name, owner, width=4):
        phys = []
        for i in range(width):
            pn = "%s%d" % (name, i)
            self.sem(pn, "@dma")
            phys.append(pn)
        self.queues[name] = [phys, 0]

    def emit(self, eng, fn, reads=(), writes=(), waits=(), inc=None, amt=1):
        ws = list(waits)
        for r in reads:
            if r.w is not None:
                ws.append(r.w)
            if r.exclusive:
                ws.extend(t for (t, e2) in r.rs if e2 != eng)
        for r in writes:
            if r.w is not None:
                ws.append(r.w)
            ws.extend(t for (t, _e) in r.rs)
        if inc in self.queues:
            q = self.queues[inc]
            inc = q[0][q[1] % len(q[0])]
            q[1] += 1
            if self.cnt[inc]:
                ws.append((inc, self.cnt[inc]))
        tok = None
        if inc is not None:
            assert self.owner[inc] in (eng, "@dma"), (inc, eng)
            self.cnt[inc] += amt
            tok = (inc, self.cnt[inc])
        self.ev[eng].append((fn, tuple(ws), inc, amt))
        for r in reads:
            if tok is not None:
                r.rs.append((tok, eng))
        for r in writes:
            r.w = tok
            r.rs = []
        return tok

    def queue_totals(self, name):
        return [(pn, self.cnt[pn]) for pn in self.queues[name][0]]

    def replay(self, block, sems):
        owner = self.owner
        for engname in ENGS:
            evs = self.ev[engname]
            if not evs:
                continue
            def mk(evs=evs, engname=engname):
                def body(eng):
                    waited = {}
                    for fn, ws, inc, amt in evs:
                        need = {}
                        for (sn, val) in ws:
                            if owner[sn] == engname == "tensor":
                                continue
                            if waited.get(sn, 0) < val and need.get(sn, 0) < val:
                                need[sn] = val
                        for sn, val in need.items():
                            eng.wait_ge(sems[sn], val)
                            waited[sn] = val
                        ins = fn(eng)
                        if inc is not None:
                            ins.then_inc(sems[inc], amt)
                return body
            getattr(block, engname)(mk())



def _rope_tables():
    inv = 1.0 / (10000.0 ** (np.arange(0, L, 2, dtype=np.float64) / L))
    t = np.arange(T, dtype=np.float64)
    fr = np.outer(t, inv)
    cos = np.repeat(np.cos(fr), 2, axis=-1)[:, :L]
    sin = np.repeat(np.sin(fr), 2, axis=-1)[:, :L]
    sinp = sin.copy()
    sinp[:, 0::2] = -sin[:, 0::2]
    return cos.T.astype(np.float32), sinp.T.astype(np.float32)


def _host_prep(inp):
    d = {}
    w = inp["L0_attn_c_attn_w"]
    d["wqkv"] = np.ascontiguousarray(
        w.reshape(NFT, 128, NCT, 128).transpose(0, 3, 2, 1)).reshape(NFT, 128, 1024)
    d["bqkv"] = np.ascontiguousarray(
        inp["L0_attn_c_attn_b"].reshape(NFT, 128).T)
    w = inp["L0_attn_cproj_w"]
    d["wcproj"] = np.ascontiguousarray(
        w.reshape(NCT, 128, NCT, 128).transpose(0, 3, 2, 1)).reshape(NCT, 128, 1024)
    d["bcproj"] = np.ascontiguousarray(
        inp["L0_attn_cproj_b"].reshape(NCT, 128).T)
    for nm, key in (("wg", "L0_ffn_w_gate"), ("wu", "L0_ffn_w_up")):
        w = inp[key]
        d[nm] = np.ascontiguousarray(
            w.reshape(NFT, 128, NCT, 128).transpose(0, 3, 2, 1)).reshape(NFT, 128, 1024)
    w = inp["L0_ffn_w_down"]
    d["wd"] = np.ascontiguousarray(
        w.reshape(NCT, 128, NFT, 128).transpose(0, 3, 2, 1)).reshape(NCT, 128, 3072)

    q2lT = inp["L0_attn_q2l_w"].T.astype(np.float32)
    kv2lT = inp["L0_attn_kv2l_w"].T.astype(np.float32)
    def col_swap(m):
        s = m.copy()
        s[:, 0::2], s[:, 1::2] = m[:, 1::2], m[:, 0::2]
        return s
    def pack2(wt):
        m = np.zeros((128, 64), dtype=np.float32)
        m[0:64, 0:32] = wt
        m[64:128, 32:64] = wt
        return m
    d["wlat_q"] = np.concatenate([pack2(q2lT), pack2(col_swap(q2lT))], axis=1)
    d["wlat_kv"] = np.concatenate([pack2(kv2lT), pack2(col_swap(kv2lT))], axis=1)
    d["wlat_v"] = pack2(kv2lT)
    def swap_b(b):
        s = b.copy()
        s[0::2], s[1::2] = b[1::2], b[0::2]
        return s
    blat = np.zeros((128, 4), dtype=np.float32)
    qb, kb = inp["L0_attn_q2l_b"], inp["L0_attn_kv2l_b"]
    blat[0:32, 0] = qb; blat[32:64, 0] = qb
    blat[64:96, 0] = swap_b(qb); blat[96:128, 0] = swap_b(qb)
    blat[0:32, 1] = kb; blat[32:64, 1] = kb
    blat[64:96, 1] = swap_b(kb); blat[96:128, 1] = swap_b(kb)
    blat[0:32, 2] = kb; blat[32:64, 2] = kb
    d["blat"] = blat
    wqk = np.zeros((64, 64), dtype=np.float32)
    wqk[0:32, 0:32] = inp["L0_attn_wqk_w"].T
    wqk[32:64, 32:64] = inp["L0_attn_wqk_w"].T
    d["wwqk"] = wqk
    bwqk = np.zeros((64, 1), dtype=np.float32)
    bwqk[0:32, 0] = inp["L0_attn_wqk_b"]
    bwqk[32:64, 0] = inp["L0_attn_wqk_b"]
    d["bwqk"] = bwqk
    l2o = np.zeros((64, 128), dtype=np.float32)
    l2o[0:32, 0:64] = inp["L0_attn_l2o_w"].T
    l2o[32:64, 64:128] = inp["L0_attn_l2o_w"].T
    d["wl2o"] = l2o
    d["bl2o"] = np.concatenate([inp["L0_attn_l2o_b"],
                                inp["L0_attn_l2o_b"]]).reshape(128, 1).astype(np.float32)

    d["ident"] = np.eye(128, dtype=np.float32)
    kk = np.arange(128).reshape(128, 1)
    qq = np.arange(128).reshape(1, 128)
    d["mask01"] = (kk <= qq).astype(np.float32)
    cosT, sinpT = _rope_tables()
    d["costab"] = np.concatenate([cosT, cosT], axis=0)
    d["sintab"] = np.concatenate([sinpT, sinpT], axis=0)
    onesA = np.zeros((1, 128), dtype=np.float32); onesA[0, 0:64] = 1.0
    onesB = np.zeros((1, 128), dtype=np.float32); onesB[0, 64:128] = 1.0
    d["onesA"] = onesA
    d["onesB"] = onesB
    for k in d:
        if d[k].dtype == np.float64:
            d[k] = d[k].astype(np.float32)
    return d


_SHARED_SPECS = [
    ("wqkv", [NFT, 128, 1024], F32), ("bqkv", [128, NFT], F32),
    ("wcproj", [NCT, 128, 1024], F32), ("bcproj", [128, NCT], F32),
    ("wg", [NFT, 128, 1024], I32), ("wu", [NFT, 128, 1024], I32),
    ("wd", [NCT, 128, 3072], I32),
    ("wlat_q", [128, 128], F32), ("wlat_kv", [128, 128], F32),
    ("wlat_v", [128, 64], F32),
    ("blat", [128, 4], F32), ("wwqk", [64, 64], F32), ("bwqk", [64, 1], F32),
    ("wl2o", [64, 128], F32), ("bl2o", [128, 1], F32),
    ("ident", [128, 128], F32), ("mask01", [128, 128], F32),
    ("costab", [64, T], F32), ("sintab", [64, T], F32),
    ("onesA", [1, 128], F32), ("onesB", [1, 128], F32),
]



def build_nc(taps=(), upto="p5", ngroups=NG, p1stage="full", gmode="full"):
    nc = bass.Bass("TRN2", target_bir_lowering=False, debug=False)
    taps = set(taps)

    dram = {}
    dram["x"] = nc.dram_tensor("x", [T, C], F32, kind="ExternalInput").ap()
    dram["xT"] = nc.dram_tensor("xT", [C, T], F32, kind="ExternalInput").ap()
    for nm, shp, dt in _SHARED_SPECS:
        dram[nm] = nc.dram_tensor(nm, shp, dt, kind="ExternalInput").ap()
    out_h = nc.dram_tensor("out", [C, T], F32, kind="ExternalOutput").ap()
    dbg = {}
    def tapout(name, shape, dt):
        if name in taps:
            if dt == BF16:
                dbg[name] = nc.dram_tensor(
                    "dbg_" + name, [shape[0], shape[1] // 2], F32,
                    kind="ExternalOutput").ap().bitcast(BF16)
            else:
                dbg[name] = nc.dram_tensor("dbg_" + name, shape, dt,
                                           kind="ExternalOutput").ap()

    tapout("xnT", [128, NCT * T], BF16)
    tapout("qkvt", [128, 3 * T], BF16)
    tapout("aug_q", [64, T], BF16)
    tapout("aug_k", [64, T], BF16)
    tapout("aug_v", [128, 16 * 66], BF16)
    tapout("ylr", [64, 512], BF16)
    tapout("latq0", [128, 512], F32)
    tapout("latk0", [128, 512], F32)
    tapout("ropeq0", [64, 512], BF16)
    tapout("vbuf0", [64, 512], BF16)
    tapout("qkvt_end", [128, T], BF16)
    tapout("rcp", [1, 1024], F32)
    tapout("yT", [128, NG * T], BF16)
    tapout("s_b", [128, NCT * T], BF16)
    tapout("h", [128, NFT * T], BF16)

    em = EM()
    em.dma_queue("dL", "sync")
    em.dma_queue("dG", "gpsimd")
    em.sem("pe", "tensor")
    em.sem("gp", "gpsimd")
    em.sem("dv", "vector")
    em.sem("sc", "scalar")

    es = ExitStack()
    with es:
        sb = {}
        def alloc(name, shape, dt):
            t = es.enter_context(nc.sbuf_tensor(name, shape, dt))
            sb[name] = t
            return t

        ident_b = alloc("ident_b", [128, 128], BF16)
        mask01_b = alloc("mask01_b", [128, 128], BF16)
        wlat_q_b = alloc("wlat_q_b", [128, 128], BF16)
        wlat_kv_b = alloc("wlat_kv_b", [128, 128], BF16)
        wlat_v_b = alloc("wlat_v_b", [128, 64], BF16)
        wwqk_b = alloc("wwqk_b", [64, 64], BF16)
        wl2o_b = alloc("wl2o_b", [64, 128], BF16)
        onesA_f = alloc("onesA_f", [1, 128], F32)
        onesB_f = alloc("onesB_f", [1, 128], F32)
        bqkv_f = alloc("bqkv_f", [128, NFT], F32)
        blat_f = alloc("blat_f", [128, 4], F32)
        bwqk_f = alloc("bwqk_f", [64, 1], F32)
        bl2o_f = alloc("bl2o_f", [128, 1], F32)
        bcproj_f = alloc("bcproj_f", [128, NCT], F32)

        xnT = alloc("xnT", [128, NCT * T], BF16)
        s_b = alloc("s_b", [128, NCT * T], BF16)
        arena1 = alloc("arena1", [128, NFT * T], BF16)
        arena2 = alloc("arena2", [128, 8 * T], BF16)
        wbuf_i = alloc("wbuf_i", [128, 1024], I32)

        wbuf_b = alloc("wbuf_b", [128, 2048], BF16)

        rtmp0 = alloc("rtmp0", [128, 512], F32)
        rtmp1 = alloc("rtmp1", [128, 512], F32)
        xn_bufs = [rtmp0[:].bitcast(BF16), rtmp1[:].bitcast(BF16)]
        st_f = alloc("st_f", [128, 160], F32)

        a1 = arena1[:]
        off = [0]
        def carve(cols):
            o = off[0]
            off[0] += cols
            assert off[0] <= NFT * T, off[0]
            return a1[:, o:o + cols]
        cstab = carve(4096).bitcast(F32)
        cos_f = cstab[0:64, :]
        sin_f = cstab[64:128, :]
        qkvt = [{k: carve(T) for k in "qkv"} for _ in range(2)]
        aug_q = [carve(T) for _ in range(2)]
        aug_k = [carve(T) for _ in range(2)]
        aug_v = [carve(16 * 66) for _ in range(2)]
        latq = [carve(1024).bitcast(F32) for _ in range(2)]
        latk = [carve(1024).bitcast(F32) for _ in range(2)]
        rvb = [carve(512) for _ in range(4)]
        ropeq = [rvb[ch][0:64, :] for ch in range(4)]
        vbuf = [rvb[ch][64:128, :] for ch in range(4)]
        pt_A = [carve(512) for _ in range(3)]
        pt_B = [carve(512) for _ in range(3)]
        ylr = carve(512)
        rsc = carve(2048).bitcast(F32)
        rsrow = rsc[0:1, :]
        rlrow = rsc[32:33, :]
        rb_f = carve(1024).bitcast(F32)
        ytmp_f = carve(1024).bitcast(F32)
        wf0 = carve(2048).bitcast(F32)
        wf1 = carve(2048).bitcast(F32)
        wpb0 = carve(1024)
        wpb1 = carve(1024)
        yT = arena2[:, 0:NG * T]
        gate_b = arena2[:, 0:T]
        up_b = arena2[:, T:2 * T]
        wd_i = arena2[:, 2 * T:2 * T + 6144].bitcast(I32)
        wbuf_i2 = arena2[:, 2 * T:2 * T + 2048].bitcast(I32)
        wd_b = arena2[:, 2 * T + 6144:2 * T + 9216]
        gbuf_b = arena2[:, 2 * T + 9216:2 * T + 9728]
        tbuf_b = arena2[:, 2 * T + 9728:2 * T + 10240]
        xnT_f = xnT[:].bitcast(F32)
        xt_p = [xnT_f[:, 0:2048], xnT_f[:, 2048:4096]]
        ob_p = xt_p
        wd_b2 = xnT[:, 8192:11264]

        PA = es.enter_context(nc.psum_tensor("PA", [128, 512], F32))[:]
        PB = es.enter_context(nc.psum_tensor("PB", [128, 512], F32))[:]
        PD = es.enter_context(nc.psum_tensor("PD", [128, 512], F32))[:]
        SA0 = es.enter_context(nc.psum_tensor("SA0", [128, 512], F32))[:]
        SA1 = es.enter_context(nc.psum_tensor("SA1", [128, 512], F32))[:]
        SBo = es.enter_context(nc.psum_tensor("SBo", [128, 512], F32))[:]
        PG = es.enter_context(nc.psum_tensor("PG", [128, 512], F32))[:]
        PT = es.enter_context(nc.psum_tensor("PT", [128, 1024], BF16))[:]

        sems = {}
        for snm in list(em.cnt):
            sems[snm] = es.enter_context(nc.semaphore(snm))

        R = {}
        def res(name):
            if name not in R:
                R[name] = Res(name, exclusive=name in
                              ("PA", "PB", "PD", "SA0", "SA1", "SBo", "PG", "PT"))
            return R[name]
        rPA, rPB, rPD, rSA0, rSA1, rSBo, rPG, rPT = (res(n) for n in
            ("PA", "PB", "PD", "SA0", "SA1", "SBo", "PG", "PT"))

        def alias_seed(dst, *srcs):
            for s in srcs:
                if s.w is not None:
                    dst.rs.append((s.w, "@alias"))
                dst.rs.extend(s.rs)

        E = em.emit

        def ld(dst, src, rdst):
            return E("sync", lambda e, d=dst, s=src: e.dma_start(out=d, in_=s),
                     writes=[rdst], inc="dL", amt=16)

        def cast(dst, src, rd, wr):
            return E("vector", lambda e, d=dst, s=src: e.tensor_copy(d, s),
                     reads=[rd], writes=[wr], inc="dv")

        ld(wf0[:, 0:128], dram["ident"], res("wf0"))
        cast(ident_b[:], wf0[:, 0:128], res("wf0"), res("ident_b"))
        ld(wf0[:, 0:128], dram["mask01"], res("wf0"))
        cast(mask01_b[:], wf0[:, 0:128], res("wf0"), res("mask01_b"))
        ld(wf0[:, 0:128], dram["wlat_q"], res("wf0"))
        cast(wlat_q_b[:], wf0[:, 0:128], res("wf0"), res("wlat_q_b"))
        ld(wf1[:, 0:128], dram["wlat_kv"], res("wf1"))
        cast(wlat_kv_b[:], wf1[:, 0:128], res("wf1"), res("wlat_kv_b"))
        ld(wf0[:, 0:64], dram["wlat_v"], res("wf0"))
        cast(wlat_v_b[:], wf0[0:128, 0:64], res("wf0"), res("wlat_v_b"))
        ld(wf1[0:64, 0:64], dram["wwqk"], res("wf1"))
        cast(wwqk_b[:], wf1[0:64, 0:64], res("wf1"), res("wwqk_b"))
        ld(wf0[0:64, 0:128], dram["wl2o"], res("wf0"))
        cast(wl2o_b[:], wf0[0:64, 0:128], res("wf0"), res("wl2o_b"))
        ld(onesA_f[:], dram["onesA"], res("onesA"))
        ld(onesB_f[:], dram["onesB"], res("onesB"))
        ld(bqkv_f[:], dram["bqkv"], res("bqkv"))
        ld(blat_f[:], dram["blat"], res("blat"))
        ld(bwqk_f[:], dram["bwqk"], res("bwqk"))
        ld(bl2o_f[:], dram["bl2o"], res("bl2o"))
        ld(bcproj_f[:], dram["bcproj"], res("bcproj"))
        ld(cos_f, dram["costab"], res("cos"))
        ld(sin_f, dram["sintab"], res("sin"))
        for pp in range(2):
            rav0 = res("aug_v%d" % pp)
            E("vector", lambda e, pp=pp: e.memset(aug_v[pp], 0.0),
              writes=[rav0], inc="dv")
            for col in (32, 65):
                ap = aug_v[pp].rearrange("p (k m) -> p k m", k=16)[:, :, col:col + 1]
                E("vector", lambda e, ap=ap: e.memset(ap, 1.0),
                  writes=[rav0], inc="dv")

        rxnT = res("xnT")
        epscol = st_f[:, 159:160]
        E("vector", lambda e: e.memset(epscol, EPS), writes=[res("eps")], inc="dv")
        for tt in range(NT):
            bi = tt % 2
            wfp = wf0 if bi == 0 else wf1
            rwf = res("wf0" if bi == 0 else "wf1")
            xn_b = xn_bufs[tt % 2]
            jk_b = xn_b
            rxn = res("xn_b%d" % (tt % 2))
            st = st_f[:, tt * 8:tt * 8 + 8]
            rst = res("st%d" % tt)
            E("sync", lambda e, d=wfp, s=dram["x"][tt * 128:(tt + 1) * 128, :]:
              e.dma_start(out=d, in_=s), writes=[rwf], inc="dL", amt=16)
            E("vector", lambda e, s=wfp, st=st: e.reduce_sum(st[:, 0:1], s, axis=AX.X),
              reads=[rwf], writes=[rst], inc="dv")
            E("scalar", lambda e, s=wfp, st=st, jk_b=jk_b: e.activation(
                jk_b, s, AF.Square, accum_out=st[:, 1:2]),
              reads=[rwf], writes=[rxn, rst], inc="sc")
            E("vector", lambda e, st=st: e.tensor_scalar(
                st[:, 2:3], st[:, 0:1], st[:, 0:1], 1.0 / (C * C), ALU.mult, ALU.mult),
              reads=[rst], writes=[rst], inc="dv")
            E("vector", lambda e, st=st: e.tensor_scalar(
                st[:, 3:4], st[:, 1:2], 1.0 / C, st[:, 2:3], ALU.mult, ALU.subtract),
              reads=[rst], writes=[rst], inc="dv")
            E("scalar", lambda e, st=st: e.activation(st[:, 4:5], st[:, 3:4], AF.Ln,
                                                      bias=epscol),
              reads=[rst, res("eps")], writes=[rst], inc="sc")
            E("scalar", lambda e, st=st: e.activation(st[:, 5:6], st[:, 4:5], AF.Exp,
                                                      scale=-0.5),
              reads=[rst], writes=[rst], inc="sc")
            E("vector", lambda e, st=st: e.tensor_scalar(
                st[:, 6:7], st[:, 0:1], st[:, 5:6], 1.0 / C, ALU.mult, ALU.mult),
              reads=[rst], writes=[rst], inc="dv")
            E("vector", lambda e, s=wfp, st=st, xn_b=xn_b: e.tensor_scalar(
                xn_b[:], s, st[:, 5:6], st[:, 6:7], ALU.mult, ALU.subtract),
              reads=[rwf, rst], writes=[rxn], inc="dv")
            for ct in range(NCT):
                E("tensor", lambda e, ct=ct, xn_b=xn_b: e.transpose(
                    PT[:, ct * 128:(ct + 1) * 128], xn_b[:, ct * 128:(ct + 1) * 128],
                    ident_b[:]),
                  reads=[rxn, res("ident_b")], writes=[rPT], inc="pe")
            xr = xnT[:].rearrange("p (c t) -> p c t", c=NCT)[:, :, tt * 128:(tt + 1) * 128]
            E("scalar", lambda e, xr=xr: e.copy(xr, PT[:, 0:1024].rearrange(
                "p (c t) -> p c t", c=NCT)),
              reads=[rPT], writes=[rxnT, res("xnT%d" % (tt // 4))], inc="sc")
        if "xnT" in dbg:
            E("gpsimd", lambda e: e.dma_start(out=dbg["xnT"], in_=xnT[:]),
              reads=[rxnT], inc="dG", amt=16)

        alias_seed(res("rtmp0"), res("xn_b0"))
        alias_seed(res("rtmp1"), res("xn_b1"))
        ryT = res("yT")
        pending = [None]
        pending_sc = []

        def emit_qkv_prep(g):
            p = g % 2
            rq = res("qkvt_q%d" % p)
            rk = res("qkvt_k%d" % p)
            rv = res("qkvt_v%d" % p)
            raq = res("aug_q%d" % p)
            rak = res("aug_k%d" % p)
            rav = res("aug_v%d" % p)

            if gmode == "none" and g > 0:
                return
            for kind, (ofi, qt_dst, rdst) in enumerate(
                    ((g, qkvt[p]["q"], rq), (8 + g, qkvt[p]["k"], rk),
                     (16 + g, qkvt[p]["v"], rv))):
                wfp = wf0 if kind % 2 == 0 else wf1
                rwf = res("wf0" if kind % 2 == 0 else "wf1")
                wpb = wpb0 if kind % 2 == 0 else wpb1
                rwpb_lo = res("wpb%d_lo" % (kind % 2))
                rwpb_hi = res("wpb%d_hi" % (kind % 2))
                E("sync", lambda e, d=wfp, s=dram["wqkv"][ofi]:
                  e.dma_start(out=d, in_=s), writes=[rwf], inc="dL", amt=16)
                E("vector", lambda e, d=wpb, s=wfp: e.tensor_copy(
                    d[:, 0:512], s[:, 0:512]),
                  reads=[rwf], writes=[rwpb_lo], inc="dv")
                E("vector", lambda e, d=wpb, s=wfp: e.tensor_copy(
                    d[:, 512:1024], s[:, 512:1024]),
                  reads=[rwf], writes=[rwpb_hi], inc="dv")
                if gmode == "dma" and g > 0:
                    continue
                for ch in range(4):
                    ps, rps = (PA, rPA) if ch % 2 == 0 else (PB, rPB)
                    for c in range(NCT):
                        E("tensor", lambda e, ps=ps, c=c, ch=ch, wpb=wpb: e.matmul(
                            ps[:], wpb[:, c * 128:(c + 1) * 128],
                            xnT[:, c * T + ch * 512: c * T + ch * 512 + 512],
                            start=(c == 0), stop=(c == NCT - 1),
                            skip_group_check=True),
                          reads=[rwpb_lo if c < 4 else rwpb_hi,
                                 res("xnT%d" % ch)], writes=[rps], inc="pe")
                    if gmode == "mm" and g > 0:
                        continue
                    E("vector", lambda e, ps=ps, d=qt_dst, ch=ch, ofi=ofi:
                      e.tensor_scalar(d[:, ch * 512:(ch + 1) * 512], ps[:],
                                      bqkv_f[:, ofi:ofi + 1], None, ALU.add),
                      reads=[rps, res("bqkv")], writes=[rdst], inc="dv")
            if "qkvt" in dbg and g == 0:
                for i, knm in enumerate("qkv"):
                    E("gpsimd", lambda e, src=qkvt[p][knm], i=i:
                      e.dma_start(out=dbg["qkvt"][:, i * T:(i + 1) * T], in_=src),
                      reads=[(rq, rk, rv)[i]], inc="dG", amt=16)

            if p1stage < "b" or (gmode == "qkv" and g > 0):
                return
            for ch in range(4):
                cp = ch % 2
                sl = slice(ch * 512, (ch + 1) * 512)
                lq, rlq = latq[cp], res("latq%d" % cp)
                lk, rlk = latk[cp], res("latk%d" % cp)
                rp_, rrp = ropeq[ch], res("ropeq%d" % ch)
                vb, rvb_ = vbuf[ch], res("vbuf%d" % ch)
                rt0, rt1 = res("rtmp0"), res("rtmp1")
                QL, rQL = (PA, rPA) if cp == 0 else (PB, rPB)
                KL, rKL = (PG, rPG) if cp == 0 else (PD, rPD)
                VL, rVL = (SA0, rSA0) if cp == 0 else (SA1, rSA1)
                E("tensor", lambda e, sl=sl, p=p, QL=QL: e.matmul(
                    QL[:], wlat_q_b[:], qkvt[p]["q"][:, sl], start=True, stop=True,
                    skip_group_check=True),
                  reads=[res("wlat_q_b"), rq], writes=[rQL], inc="pe")
                E("tensor", lambda e, sl=sl, p=p, KL=KL: e.matmul(
                    KL[:], wlat_kv_b[:], qkvt[p]["k"][:, sl], start=True, stop=True,
                    skip_group_check=True),
                  reads=[res("wlat_kv_b"), rk], writes=[rKL], inc="pe")
                E("tensor", lambda e, sl=sl, p=p, VL=VL: e.matmul(
                    VL[0:64, :], wlat_v_b[:], qkvt[p]["v"][:, sl], start=True,
                    stop=True, skip_group_check=True),
                  reads=[res("wlat_v_b"), rv], writes=[rVL], inc="pe")
                E("scalar", lambda e, lq=lq, QL=QL: e.activation(
                    lq[:, 0:512], QL[:], AF.Identity, bias=blat_f[:, 0:1]),
                  reads=[rQL, res("blat")], writes=[rlq], inc="sc")
                E("vector", lambda e, sl=sl, lq=lq: e.tensor_mul(
                    rtmp0[0:64, :], lq[0:64, 0:512], cos_f[:, sl]),
                  reads=[rlq, res("cos")], writes=[rt0], inc="dv")
                E("vector", lambda e, sl=sl, lq=lq: e.tensor_mul(
                    rtmp1[0:64, :], lq[64:128, 0:512], sin_f[:, sl]),
                  reads=[rlq, res("sin")], writes=[rt1], inc="dv")
                E("vector", lambda e, rp_=rp_: e.tensor_add(
                    rp_[:, 0:512], rtmp0[0:64, :], rtmp1[0:64, :]),
                  reads=[rt0, rt1], writes=[rrp], inc="dv")
                E("scalar", lambda e, lk=lk, KL=KL: e.activation(
                    lk[:, 0:512], KL[:], AF.Identity, bias=blat_f[:, 1:2]),
                  reads=[rKL, res("blat")], writes=[rlk], inc="sc")
                E("vector", lambda e, sl=sl, lk=lk: e.tensor_mul(
                    rtmp0[0:64, :], lk[0:64, 0:512], cos_f[:, sl]),
                  reads=[rlk, res("cos")], writes=[rt0], inc="dv")
                E("vector", lambda e, sl=sl, lk=lk: e.tensor_mul(
                    rtmp1[0:64, :], lk[64:128, 0:512], sin_f[:, sl]),
                  reads=[rlk, res("sin")], writes=[rt1], inc="dv")
                E("vector", lambda e, sl=sl, p=p: e.tensor_add(
                    aug_k[p][0:64, sl], rtmp0[0:64, :], rtmp1[0:64, :]),
                  reads=[rt0, rt1], writes=[rak], inc="dv")
                E("scalar", lambda e, vb=vb, VL=VL: e.activation(
                    vb[:, 0:512], VL[0:64, :], AF.Identity,
                    bias=blat_f[0:64, 2:3]),
                  reads=[rVL, res("blat")], writes=[rvb_], inc="sc")

                if "latq0" in dbg and g == 0 and ch == 0:
                    E("gpsimd", lambda e, lq=lq: e.dma_start(out=dbg["latq0"], in_=lq[:, 0:512]),
                      reads=[rlq], inc="dG", amt=16)
                if "latk0" in dbg and g == 0 and ch == 0:
                    E("gpsimd", lambda e, lk=lk: e.dma_start(out=dbg["latk0"], in_=lk[:, 0:512]),
                      reads=[rlk], inc="dG", amt=16)
                if "ropeq0" in dbg and g == 0 and ch == 0:
                    E("gpsimd", lambda e, rp_=rp_: e.dma_start(out=dbg["ropeq0"], in_=rp_[:, 0:512]),
                      reads=[rrp], inc="dG", amt=16)
                if "vbuf0" in dbg and g == 0 and ch == 0:
                    E("gpsimd", lambda e, vb=vb: e.dma_start(out=dbg["vbuf0"], in_=vb[:, 0:512]),
                      reads=[rvb_], inc="dG", amt=16)

        def emit_prep_b(g):
            p = g % 2
            raq = res("aug_q%d" % p)
            rav = res("aug_v%d" % p)
            wbank = ((PA, rPA), (PB, rPB), (PG, rPG), (PD, rPD))
            for ch in range(4):
                rp_, rrp = ropeq[ch], res("ropeq%d" % ch)
                WB, rWB = wbank[ch]
                E("tensor", lambda e, rp_=rp_, WB=WB: e.matmul(
                    WB[0:64, :], wwqk_b[:], rp_[:, 0:512],
                    start=True, stop=True, skip_group_check=True),
                  reads=[res("wwqk_b"), rrp], writes=[rWB], inc="pe")
            for ch in range(4):
                vb, rvb_ = vbuf[ch], res("vbuf%d" % ch)
                for kt in range(4):
                    E("tensor", lambda e, kt=kt, vb=vb, ch=ch: e.transpose(
                        PT[:, ch * 256 + kt * 64: ch * 256 + (kt + 1) * 64],
                        vb[:, kt * 128:(kt + 1) * 128], ident_b[64:128, 64:128]),
                      reads=[rvb_, res("ident_b")], writes=[rPT], inc="pe")
            for ch in range(4):
                sl = slice(ch * 512, (ch + 1) * 512)
                WB, rWB = wbank[ch]
                E("scalar", lambda e, sl=sl, p=p, WB=WB: e.activation(
                    aug_q[p][0:64, sl], WB[0:64, :], AF.Identity, bias=bwqk_f[:]),
                  reads=[rWB, res("bwqk")], writes=[raq], inc="sc")
                srcT = PT[:, ch * 256:ch * 256 + 256].rearrange(
                    "p (k m) -> p k m", k=4)
                dstA = aug_v[p].rearrange("p (k m) -> p k m", k=16)[
                    :, ch * 4:(ch + 1) * 4, 0:32]
                dstB = aug_v[p].rearrange("p (k m) -> p k m", k=16)[
                    :, ch * 4:(ch + 1) * 4, 33:65]
                E("vector", lambda e, d=dstA, s=srcT[:, :, 0:32]: e.tensor_copy(d, s),
                  reads=[rPT], writes=[rav], inc="dv")
                E("vector", lambda e, d=dstB, s=srcT[:, :, 32:64]: e.tensor_copy(d, s),
                  reads=[rPT], writes=[rav], inc="dv")
            if "aug_q" in dbg and g == 0:
                E("gpsimd", lambda e, p=p: e.dma_start(out=dbg["aug_q"], in_=aug_q[p][0:64, :]),
                  reads=[raq], inc="dG", amt=16)
            if "aug_k" in dbg and g == 0:
                E("gpsimd", lambda e, p=p: e.dma_start(out=dbg["aug_k"], in_=aug_k[p][0:64, :]),
                  reads=[res("aug_k%d" % p)], inc="dG", amt=16)
            if "aug_v" in dbg and g == 0:
                E("gpsimd", lambda e, p=p: e.dma_start(out=dbg["aug_v"], in_=aug_v[p]),
                  reads=[rav], inc="dG", amt=16)

        def emit_pass3(g):
            if p1stage < "c":
                return
            p = g % 2
            raq = res("aug_q%d" % p)
            rak = res("aug_k%d" % p)
            rav = res("aug_v%d" % p)
            for j in range(4):
                ni = 4 * j + 4
                avq = []
                for i in range(ni):
                    qoff = max(512 * j, 128 * i)
                    N = 512 * j + 512 - qoff
                    ip = i % 3
                    rpa, rpb = res("pt_A%d" % ip), res("pt_B%d" % ip)
                    SAp, rSAp = (SA0, rSA0) if i % 2 == 0 else (SA1, rSA1)
                    E("tensor", lambda e, i=i, qoff=qoff, N=N, p=p, SAp=SAp: e.matmul(
                        SAp[:, 0:N], aug_k[p][0:32, i * 128:(i + 1) * 128],
                        aug_q[p][0:32, qoff:qoff + N],
                        start=True, stop=True, skip_group_check=True),
                      reads=[rak, raq], writes=[rSAp], inc="pe")
                    SBp, rSBp = (SBo, rSBo) if i % 2 == 0 else (PB, rPB)
                    E("tensor", lambda e, i=i, qoff=qoff, N=N, p=p, SBp=SBp: e.matmul(
                        SBp[:, 0:N], aug_k[p][32:64, i * 128:(i + 1) * 128],
                        aug_q[p][32:64, qoff:qoff + N],
                        start=True, stop=True, skip_group_check=True),
                      reads=[rak, raq], writes=[rSBp], inc="pe")
                    E("scalar", lambda e, N=N, ip=ip, SAp=SAp: e.activation(
                        pt_A[ip][:, 0:N], SAp[:, 0:N], AF.Exp, scale=SCL),
                      reads=[rSAp], writes=[rpa], inc="sc")
                    E("scalar", lambda e, N=N, ip=ip, SBp=SBp: e.activation(
                        pt_B[ip][:, 0:N], SBp[:, 0:N], AF.Exp, scale=SCL),
                      reads=[rSBp], writes=[rpb], inc="sc")
                    if qoff == 128 * i:
                        E("vector", lambda e, ip=ip: e.tensor_mul(
                            pt_A[ip][:, 0:128], pt_A[ip][:, 0:128], mask01_b[:]),
                          reads=[res("mask01_b"), rpa], writes=[rpa], inc="dv")
                        E("vector", lambda e, ip=ip: e.tensor_mul(
                            pt_B[ip][:, 0:128], pt_B[ip][:, 0:128], mask01_b[:]),
                          reads=[res("mask01_b"), rpb], writes=[rpb], inc="dv")
                    def em_av(i=i, j=j, qoff=qoff, N=N, ip=ip, p=p,
                              rpa=rpa, rpb=rpb):
                        E("tensor", lambda e: e.matmul(
                            PG[0:33, qoff - 512 * j:qoff - 512 * j + N],
                            aug_v[p][:, i * 66:i * 66 + 33], pt_A[ip][:, 0:N],
                            start=(i == 0), stop=(i == 4 * j + 3),
                            skip_group_check=True),
                          reads=[rav, rpa], writes=[rPG], inc="pe")
                        E("tensor", lambda e: e.matmul(
                            PD[0:33, qoff - 512 * j:qoff - 512 * j + N],
                            aug_v[p][:, i * 66 + 33:(i + 1) * 66], pt_B[ip][:, 0:N],
                            start=(i == 0), stop=(i == 4 * j + 3),
                            skip_group_check=True),
                          reads=[rav, rpb], writes=[rPD], inc="pe")
                    avq.append(em_av)
                    if i in (2, 3) and pending_sc:
                        pending_sc.pop(0)()
                    if i >= 2:
                        avq[i - 2]()
                if ni >= 2:
                    avq[ni - 2]()
                avq[ni - 1]()
                if p1stage < "d":
                    continue
                if pending[0] is not None:
                    pending[0]()
                    pending[0] = None
                rrs, rrl = res("rsrow"), res("rlrow")
                rylr = res("ylr")
                E("vector", lambda e: e.tensor_copy(rsrow[:, 0:512], PG[32:33, :]),
                  reads=[rPG], writes=[rrs], inc="dv")
                E("vector", lambda e: e.tensor_copy(rsrow[:, 512:1024], PD[32:33, :]),
                  reads=[rPD], writes=[rrs], inc="dv")
                def em_ln(rrs=rrs, rrl=rrl):
                    E("scalar", lambda e: e.activation(rlrow[:, 0:1024],
                                                       rsrow[:, 0:1024], AF.Ln),
                      reads=[rrs], writes=[rrl], inc="sc")
                def em_exp(rrs=rrs, rrl=rrl):
                    E("scalar", lambda e: e.activation(rsrow[:, 0:1024],
                                                       rlrow[:, 0:1024], AF.Exp,
                                                       scale=-1.0),
                      reads=[rrl], writes=[rrs], inc="sc")
                pending_sc.clear()
                pending_sc.extend([em_ln, em_exp])
                if "rcp" in dbg and g == 0 and j == 0:
                    E("gpsimd", lambda e: e.dma_start(out=dbg["rcp"],
                                                      in_=rsrow[:, 0:1024]),
                      reads=[rrs], inc="dG", amt=16)
                E("vector", lambda e: e.tensor_copy(ylr[0:32, 0:512], PG[0:32, :]),
                  reads=[rPG], writes=[rylr], inc="dv")
                E("vector", lambda e: e.tensor_copy(ylr[32:64, 0:512], PD[0:32, :]),
                  reads=[rPD], writes=[rylr], inc="dv")
                if "ylr" in dbg and g == 0 and j == 0:
                    E("gpsimd", lambda e: e.dma_start(out=dbg["ylr"],
                                                      in_=ylr[0:64, 0:512]),
                      reads=[rylr], inc="dG", amt=16)
                def em_epi(g=g, j=j, rrs=rrs, rylr=rylr):
                    E("tensor", lambda e: e.matmul(
                        SA0[:], onesA_f[:], rsrow[:, 0:512],
                        start=True, stop=False, skip_group_check=True),
                      reads=[res("onesA"), rrs], writes=[rSA0], inc="pe")
                    E("tensor", lambda e: e.matmul(
                        SA0[:], onesB_f[:], rsrow[:, 512:1024],
                        start=False, stop=True, skip_group_check=True),
                      reads=[res("onesB"), rrs], writes=[rSA0], inc="pe")
                    rrb = res("rb_f")
                    E("vector", lambda e: e.tensor_copy(rb_f[:, 0:512], SA0[:]),
                      reads=[rSA0], writes=[rrb], inc="dv")
                    E("tensor", lambda e: e.matmul(
                        SA1[:], wl2o_b[:], ylr[0:64, 0:512], start=True, stop=True,
                        skip_group_check=True),
                      reads=[res("wl2o_b"), rylr], writes=[rSA1], inc="pe")
                    ryt = res("ytmp_f")
                    E("vector", lambda e: e.tensor_mul(ytmp_f[:, 0:512], SA1[:],
                                                       rb_f[:, 0:512]),
                      reads=[rSA1, rrb], writes=[ryt], inc="dv")
                    E("vector", lambda e: e.tensor_scalar(
                        yT[:, g * T + j * 512: g * T + (j + 1) * 512],
                        ytmp_f[:, 0:512], bl2o_f[:], None, ALU.add),
                      reads=[ryt, res("bl2o")], writes=[ryT], inc="dv")
                pending[0] = em_epi

        if upto >= "p1":
            for g in range(ngroups):
                emit_qkv_prep(g)
                if g >= 1:
                    emit_pass3(g - 1)
                emit_prep_b(g)
            emit_pass3(ngroups - 1)
        while pending_sc:
            pending_sc.pop(0)()
        if pending[0] is not None:
            pending[0]()
            pending[0] = None
        if "qkvt_end" in dbg:
            E("gpsimd", lambda e: e.dma_start(out=dbg["qkvt_end"], in_=qkvt[0]["q"]),
              reads=[res("qkvt_q0")], inc="dG", amt=16)
        if "yT" in dbg:
            E("gpsimd", lambda e: e.dma_start(out=dbg["yT"], in_=yT),
              reads=[ryT], inc="dG", amt=16)

        rs_b = res("s_b")
        for dtile in (range(NCT) if upto >= "p2" and ngroups == NG else ()):
            wfp = wf0 if dtile % 2 == 0 else wf1
            rwf = res("wf0" if dtile % 2 == 0 else "wf1")
            wpb = wpb0 if dtile % 2 == 0 else wpb1
            rwpb_lo = res("wpb%d_lo" % (dtile % 2))
            rwpb_hi = res("wpb%d_hi" % (dtile % 2))
            E("sync", lambda e, d=wfp, s=dram["wcproj"][dtile]:
              e.dma_start(out=d, in_=s), writes=[rwf], inc="dL", amt=16)
            E("vector", lambda e, d=wpb, s=wfp: e.tensor_copy(
                d[:, 0:512], s[:, 0:512]),
              reads=[rwf], writes=[rwpb_lo], inc="dv")
            E("vector", lambda e, d=wpb, s=wfp: e.tensor_copy(
                d[:, 512:1024], s[:, 512:1024]),
              reads=[rwf], writes=[rwpb_hi], inc="dv")
            for ch in range(4):
                ps, rps = (PA, rPA) if ch % 2 == 0 else (PB, rPB)
                for c in range(NCT):
                    E("tensor", lambda e, ps=ps, c=c, ch=ch, wpb=wpb: e.matmul(
                        ps[:], wpb[:, c * 128:(c + 1) * 128],
                        yT[:, c * T + ch * 512: c * T + ch * 512 + 512],
                        start=(c == 0), stop=(c == NCT - 1),
                        skip_group_check=True),
                      reads=[rwpb_lo if c < 4 else rwpb_hi, ryT],
                      writes=[rps], inc="pe")
                E("scalar", lambda e, ps=ps, ch=ch, dtile=dtile: e.activation(
                    s_b[:, dtile * T + ch * 512: dtile * T + (ch + 1) * 512],
                    ps[:], AF.Identity, bias=bcproj_f[:, dtile:dtile + 1]),
                  reads=[rps, res("bcproj")], writes=[rs_b], inc="sc")
        if "s_b" in dbg:
            E("gpsimd", lambda e: e.dma_start(out=dbg["s_b"], in_=s_b[:]),
              reads=[rs_b], inc="dG", amt=16)

        rh = res("h")
        rga, rup = res("gate_b"), res("up_b")
        alias_seed(rga, ryT)
        alias_seed(rup, ryT)
        alias_seed(res("wbuf_i1"), ryT)
        a1names = (["cos", "sin", "ylr", "rsrow", "rlrow", "rb_f", "ytmp_f",
                    "wf0", "wf1", "wpb0_lo", "wpb0_hi", "wpb1_lo", "wpb1_hi"]
                   + ["qkvt_%s%d" % (k, p) for k in "qkv" for p in range(2)]
                   + ["aug_%s%d" % (k, p) for k in "qkv" for p in range(2)]
                   + ["latq0", "latq1", "latk0", "latk1", "ropeq0", "ropeq1",
                      "vbuf0", "vbuf1", "pt_A0", "pt_A1", "pt_B0", "pt_B1"])
        alias_seed(rh, *[res(n) for n in a1names])
        def p4_dma(ht, which):
            wsrc = "wg" if which == 0 else "wu"
            wbi = wbuf_i if which == 0 else wbuf_i2
            rwi = res("wbuf_i%d" % which)
            E("sync", lambda e, s=dram[wsrc][ht], wbi=wbi: e.dma_start(out=wbi[:], in_=s),
              writes=[rwi], inc="dL", amt=16)
            half = which
            rwb_lo = res("wbuf_b%d_lo" % half)
            rwb_hi = res("wbuf_b%d_hi" % half)
            wslice = wbuf_b[:, half * 1024:(half + 1) * 1024]
            E("vector", lambda e, w=wslice, wbi=wbi: e.tensor_copy(
                w[:, 0:512], wbi[:, 0:512]),
              reads=[rwi], writes=[rwb_lo], inc="dv")
            E("vector", lambda e, w=wslice, wbi=wbi: e.tensor_copy(
                w[:, 512:1024], wbi[:, 512:1024]),
              reads=[rwi], writes=[rwb_hi], inc="dv")

        jobs = ([(ht, w) for ht in range(NFT) for w in (0, 1)]
                if upto >= "p4" else [])
        if jobs:
            p4_dma(*jobs[0])
            p4_dma(*jobs[1])
        bank4 = ((PA, rPA), (PB, rPB), (SA0, rSA0), (SA1, rSA1))
        for idx, (ht, which) in enumerate(jobs):
            half = which
            rwb_lo = res("wbuf_b%d_lo" % half)
            rwb_hi = res("wbuf_b%d_hi" % half)
            wslice = wbuf_b[:, half * 1024:(half + 1) * 1024]
            for ch in range(4):
                ps, rps = bank4[ch]
                for c in range(NCT):
                    E("tensor", lambda e, ps=ps, c=c, ch=ch, w=wslice: e.matmul(
                        ps[:], w[:, c * 128:(c + 1) * 128],
                        xnT[:, c * T + ch * 512: c * T + ch * 512 + 512],
                        start=(c == 0), stop=(c == NCT - 1),
                        skip_group_check=True),
                      reads=[rwb_lo if c < 4 else rwb_hi,
                             res("xnT%d" % ch)], writes=[rps], inc="pe")
            if idx + 2 < len(jobs):
                p4_dma(*jobs[idx + 2])
            for ch in range(4):
                ps, rps = bank4[ch]
                if which == 0:
                    E("scalar", lambda e, ps=ps, ch=ch: e.activation(
                        gate_b[:, ch * 512:(ch + 1) * 512], ps[:], AF.Silu,
                        scale=S127),
                      reads=[rps], writes=[rga], inc="sc")
                else:
                    E("scalar", lambda e, ps=ps, ch=ch: e.activation(
                        up_b[:, ch * 512:(ch + 1) * 512], ps[:], AF.Identity,
                        scale=S127),
                      reads=[rps], writes=[rup], inc="sc")
            if which == 1:
                for hh in range(2):
                    E("vector", lambda e, ht=ht, hh=hh: e.tensor_mul(
                        arena1[:, ht * T + hh * 1024: ht * T + (hh + 1) * 1024],
                        gate_b[:, hh * 1024:(hh + 1) * 1024],
                        up_b[:, hh * 1024:(hh + 1) * 1024]),
                      reads=[rga, rup], writes=[rh], inc="dv")
        if "h" in dbg:
            E("gpsimd", lambda e: e.dma_start(out=dbg["h"], in_=arena1[:]),
              reads=[rh], inc="dG", amt=16)

        final_toks = []
        rwdi = res("wd_i")
        alias_seed(rwdi, ryT, res("wbuf_i1"))
        alias_seed(res("gbuf_b"), ryT)
        alias_seed(res("wd_bp0"), ryT)
        alias_seed(res("tbuf_b"), ryT)
        for nm in ("xt0", "xt1", "wd_bp1"):
            alias_seed(res(nm), rxnT, *[res("xnT%d" % w) for w in range(4)])
        def p5_dma(dtile):
            E("sync", lambda e, s=dram["wd"][dtile]: e.dma_start(out=wd_i, in_=s),
              writes=[rwdi], inc="dL", amt=16)
            wdb = wd_b if dtile % 2 == 0 else wd_b2
            rwdb = res("wd_bp%d" % (dtile % 2))
            E("vector", lambda e, wdb=wdb: e.tensor_copy(wdb, wd_i),
              reads=[rwdi], writes=[rwdb], inc="dv")

        if upto >= "p5":
            p5_dma(0)
            p5_dma(1)
        for dtile in (range(NCT) if upto >= "p5" else ()):
            wdb = wd_b if dtile % 2 == 0 else wd_b2
            rwdb = res("wd_bp%d" % (dtile % 2))
            par = dtile % 2
            rxt = res("xt%d" % par)
            E("sync", lambda e, d=xt_p[par], s=dram["xT"][dtile * 128:(dtile + 1) * 128, :]:
              e.dma_start(out=d, in_=s), writes=[rxt], inc="dL", amt=16)
            rob = rxt
            bank4 = ((PA, rPA), (PB, rPB), (SA0, rSA0), (SA1, rSA1))
            for ch in range(4):
                ps, rps = bank4[ch]
                for k in range(NFT):
                    E("tensor", lambda e, ps=ps, k=k, ch=ch, wdb=wdb: e.matmul(
                        ps[:], wdb[:, k * 128:(k + 1) * 128],
                        arena1[:, k * T + ch * 512: k * T + ch * 512 + 512],
                        start=(k == 0), stop=(k == NFT - 1),
                        skip_group_check=True),
                      reads=[rwdb, rh], writes=[rps], inc="pe")
            if dtile + 2 < NCT:
                p5_dma(dtile + 2)
            for ch in range(4):
                ps, rps = bank4[ch]
                rgb, rtb = res("gbuf_b"), res("tbuf_b")
                E("scalar", lambda e, ps=ps: e.activation(
                    gbuf_b, ps[:], AF.Silu, scale=S127),
                  reads=[rps], writes=[rgb], inc="sc")
                E("vector", lambda e, ch=ch, dtile=dtile: e.tensor_mul(
                    tbuf_b, gbuf_b,
                    s_b[:, dtile * T + ch * 512: dtile * T + (ch + 1) * 512]),
                  reads=[rgb, rs_b], writes=[rtb], inc="dv")
                E("vector", lambda e, ch=ch, par=par: e.tensor_add(
                    ob_p[par][:, ch * 512:(ch + 1) * 512],
                    xt_p[par][:, ch * 512:(ch + 1) * 512], tbuf_b),
                  reads=[rtb, rxt], writes=[rob], inc="dv")
            tk = E("sync", lambda e, par=par, dtile=dtile: e.dma_start(
                out=out_h[dtile * 128:(dtile + 1) * 128, :], in_=ob_p[par]),
              reads=[rob], inc="dL", amt=16)
            final_toks.append(tk)

        def _final_wait(e):
            ins = None
            for pn, cnt in em.queue_totals("dL"):
                if cnt:
                    ins = e.wait_ge(sems[pn], cnt)
            for pn, cnt in em.queue_totals("dG"):
                if cnt:
                    ins = e.wait_ge(sems[pn], cnt)
            return ins
        E("gpsimd", _final_wait, waits=final_toks)

        with nc.Block() as block:
            em.replay(block, sems)
    return nc, dbg



def kernel(**inputs):
    shared = _host_prep(inputs)
    x = np.asarray(inputs["L0_attn_x"])
    nc, _ = build_nc()
    in_maps = []
    for c in range(NCORES):
        m = dict(shared)
        xs = np.ascontiguousarray(x[c])
        m["x"] = xs
        m["xT"] = np.ascontiguousarray(xs.T)
        in_maps.append(m)
    res = run_bass_kernel_spmd(nc, in_maps, core_ids=list(range(NCORES)))
    out = np.stack([np.ascontiguousarray(np.asarray(r["out"]).T)
                    for r in res.results], axis=0)
    return out
